# version 02
# baseline (speedup 1.0000x reference)
import numpy as np
import ml_dtypes
import concourse.bass as bass
import concourse.mybir as mybir
from concourse.bass_utils import run_bass_kernel_spmd
from contextlib import ExitStack

F32 = mybir.dt.float32
BF16 = mybir.dt.bfloat16
AF = mybir.ActivationFunctionType
ALU = mybir.AluOpType
AX = mybir.AxisListType

D = 2048
CC = 16
TL = 2048
NH = 16
FM = 1408
FCH = 11
NEXP = 12
TH = 1024
ATT_SCALE = float(1.0 / np.sqrt(128.0))
DEBUG_AA = [False]
GROUPS = [[0, 1, 2, 3], [4, 5, 6, 7]]

_UNIQ = [0]


def _sb(nc, name, shape, dt):
    _UNIQ[0] += 1
    return nc.sbuf_tensor(f"{name}_{_UNIQ[0]}", shape, dt)


def _ps(nc, name, shape, dt):
    _UNIQ[0] += 1
    return nc.psum_tensor(f"{name}_{_UNIQ[0]}", shape, dt)


class MK:
    def __init__(self, nc, es, block, aps):
        self.nc = nc
        self.es = es
        self.blk = block
        self.a = aps
        self.S = {}
        self.c = {}
        for name in ["qs", "qc", "qg", "pe", "dv", "ac", "cc"]:
            self.S[name] = es.enter_context(nc.semaphore("sem_" + name))
            self.c[name] = 0
        self.bar = dict(self.c)
        self.bar_done = set()
        self.sb = {}

    def inc(self, name, instr, n=1):
        instr.then_inc(self.S[name], n)
        self.c[name] += n
        return self.c[name]

    def dma(self, eng, qname, out, in_):
        i = eng.dma_start(out=out, in_=in_)
        return self.inc(qname, i, 16)

    def barrier(self):
        self.bar = dict(self.c)
        self.bar_done = set()

    def w(self, eng, name, v):
        if v > 0:
            eng.wait_ge(self.S[name], v)

    def se(self, eng, instr, name="dv"):
        self.w(eng, name, self.inc(name, instr))

    def sec(self, engname):
        blkfn = getattr(self.blk, engname)

        def deco(f):
            def wrapped(e):
                if engname not in self.bar_done:
                    self.bar_done.add(engname)
                    for k, v in self.bar.items():
                        if v > 0:
                            e.wait_ge(self.S[k], v)
                f(e)
            blkfn(wrapped)
        return deco

    def alloc_persistent(self):
        nc, es, sb = self.nc, self.es, self.sb

        def t(name, shape, dt):
            sb[name] = es.enter_context(nc.sbuf_tensor("sbt_" + name, shape, dt))

        t("ones_row_f", [1, 128], F32)
        t("ones_col_b", [128, 1], BF16)
        t("id_f", [128, 128], F32)
        t("eind", [9, 9 * 128], F32)
        t("gate9", [128, CC * 9], F32)
        t("sel4", [1, 4], F32)
        t("constsA", [128, 8], F32)
        for nm in ["ba", "bx", "lam", "bq", "bk", "bo", "ngam", "bqs"]:
            t(nm, [128, 16], F32)
        t("Rb", [128, TL], F32)
        t("MRb", [128, TL], F32)

    def load_persistent(self):
        mk, sb, a = self, self.sb, self.a

        @mk.sec("sync")
        def _(e):
            for nm, src in [("ones_row_f", "ones_row_f"), ("ones_col_b", "ones_col_b"),
                            ("id_f", "id_f"), ("eind", "eind"), ("gate9", "gate9T"),
                            ("sel4", "sel"), ("ba", "ba_c"), ("bx", "bx_c"),
                            ("lam", "lam_c"), ("bq", "bq_c"), ("bk", "bk_c"),
                            ("bo", "bo_c"),
                            ("constsA", "constsA")]:
                mk.dma(e, "qs", sb[nm][:, :], a[src][:, :])
        n_loads = self.c["qs"]

        ph_ps = ExitStack()
        ps_b = ph_ps.enter_context(self.nc.psum_tensor("ps_pers", [128, 512], F32))

        @mk.sec("scalar")
        def _(e):
            mk.w(e, "qs", n_loads)
            mk.se(e, e.activation(out=sb["ngam"][:, :], in_=sb["lam"][:, :], func=AF.Exp,
                                  scale=sb["constsA"][:, 2:3]), "ac")
            mk.se(e, e.activation(out=sb["ngam"][:, :], in_=sb["ngam"][:, :], func=AF.Ln,
                                  bias=sb["constsA"][:, 0:1]), "ac")
            i = e.activation(out=sb["bqs"][:, :], in_=sb["bq"][:, :], func=AF.Identity,
                             scale=sb["constsA"][:, 3:4])
            mk.inc("ac", i)

        @mk.sec("vector")
        def _(e):
            mk.w(e, "ac", self.c["ac"])
            i = e.tensor_scalar(sb["ngam"][:, :], sb["ngam"][:, :], -8.0, None, ALU.mult)
            mk.inc("dv", i)

        self.barrier()
        ph_ps.close()

    def stats_and_rows(self, src_name):
        mk, sb, a, nc = self, self.sb, self.a, self.nc
        src = a[src_name]
        with ExitStack() as ph:
            xb = [ph.enter_context(_sb(nc, f"st_xb{i}", [128, TL], BF16)) for i in range(2)]
            xq = [ph.enter_context(_sb(nc, f"st_xq{i}", [128, TL], BF16)) for i in range(2)]
            rows = {}
            for rn in ["m_row", "v_row", "tmp_row", "rstd_row", "mr_row"]:
                rows[rn] = ph.enter_context(_sb(nc, "st_" + rn, [1, TL], F32))
            with ExitStack() as ps_stk:
                ps_sm = ps_stk.enter_context(_ps(nc, "ps_sm", [1, TL], F32))
                ps_sq = ps_stk.enter_context(_ps(nc, "ps_sq", [1, TL], F32))

                ld = {}; sq = {}; mm = {}
                for cc in range(CC):
                    @mk.sec("gpsimd")
                    def _(e, cc=cc):
                        if cc >= 2:
                            mk.w(e, "pe", mm[cc - 2])
                            mk.w(e, "ac", sq[cc - 2])
                        ld[cc] = mk.dma(e, "qg", xb[cc % 2][:, :],
                                        src[cc * 128:(cc + 1) * 128, :])

                    @mk.sec("scalar")
                    def _(e, cc=cc):
                        mk.w(e, "qg", ld[cc])
                        if cc >= 2:
                            mk.w(e, "pe", mm[cc - 2])
                        i = e.square(xq[cc % 2][:, :], xb[cc % 2][:, :])
                        sq[cc] = mk.inc("ac", i)

                    @mk.sec("tensor")
                    def _(e, cc=cc):
                        mk.w(e, "qg", ld[cc])
                        mk.w(e, "ac", sq[cc])
                        for nt in range(4):
                            e.matmul(ps_sm[0:1, nt * 512:(nt + 1) * 512],
                                     sb["ones_col_b"][:, :],
                                     xb[cc % 2][:, nt * 512:(nt + 1) * 512],
                                     start=(cc == 0), stop=(cc == CC - 1),
                                     skip_group_check=True)
                            i = e.matmul(ps_sq[0:1, nt * 512:(nt + 1) * 512],
                                         sb["ones_col_b"][:, :],
                                         xq[cc % 2][:, nt * 512:(nt + 1) * 512],
                                         start=(cc == 0), stop=(cc == CC - 1),
                                         skip_group_check=True)
                        mm[cc] = mk.inc("pe", i)

                @mk.sec("vector")
                def _(e):
                    mk.w(e, "pe", mm[CC - 1])
                    e.tensor_scalar(rows["m_row"][:, :], ps_sm[0:1, :], 1.0 / D, None, ALU.mult)
                    i = e.tensor_scalar(rows["v_row"][:, :], ps_sq[0:1, :], 1.0 / D, None,
                                        ALU.mult)
                    mk.se(e, i)
                    mk.inc("dv", e.engine_nop())

                @mk.sec("scalar")
                def _(e):
                    mk.w(e, "dv", self.c["dv"])
                    i = e.square(rows["tmp_row"][:, :], rows["m_row"][:, :])
                    mk.inc("ac", i)

                @mk.sec("vector")
                def _(e):
                    mk.w(e, "ac", self.c["ac"])
                    i = e.tensor_tensor(rows["v_row"][:, :], rows["v_row"][:, :],
                                        rows["tmp_row"][:, :], ALU.subtract)
                    mk.inc("dv", i)

                @mk.sec("scalar")
                def _(e):
                    mk.w(e, "dv", self.c["dv"])
                    i = e.activation(out=rows["tmp_row"][:, :], in_=rows["v_row"][:, :],
                                     func=AF.Sqrt, bias=sb["constsA"][0:1, 1:2])
                    mk.inc("ac", i)

                @mk.sec("vector")
                def _(e):
                    mk.w(e, "ac", self.c["ac"])
                    mk.se(e, e.reciprocal(rows["rstd_row"][:, :], rows["tmp_row"][:, :]))
                    i = e.tensor_tensor(rows["mr_row"][:, :], rows["m_row"][:, :],
                                        rows["rstd_row"][:, :], ALU.mult)
                    mk.inc("dv", i)
                self.barrier()

            with ExitStack() as ps_stk:
                ps_bc = ps_stk.enter_context(_ps(nc, "ps_bc", [128, TL], F32))
                for row, dst in [("rstd_row", "Rb"), ("mr_row", "MRb")]:
                    @mk.sec("tensor")
                    def _(e, row=row):
                        mk.w(e, "dv", self.c["dv"])
                        for nt in range(4):
                            i = e.matmul(ps_bc[:, nt * 512:(nt + 1) * 512],
                                         sb["ones_row_f"][:, :],
                                         rows[row][:, nt * 512:(nt + 1) * 512],
                                         start=True, stop=True)
                        mk.inc("pe", i)

                    @mk.sec("vector")
                    def _(e, dst=dst):
                        mk.w(e, "pe", self.c["pe"])
                        i = e.tensor_copy(sb[dst][:, :], ps_bc[:, :])
                        mk.inc("dv", i)
                self.barrier()

    def emit_xn(self, xn, src, width=TL, t0=0, extra=None, xcb=None):
        mk, sb, nc = self, self.sb, self.nc
        with ExitStack() as ph:
            if xcb is None:
                xc = [ph.enter_context(_sb(nc, f"xn_xc{i}", [128, width], F32)) for i in range(2)]
            else:
                xc = xcb
            tt = ph.enter_context(_sb(nc, "xn_tt", [128, width], F32))
            tf = ph.enter_context(_sb(nc, "xn_tf", [128, width], F32))
            dvk = {}
            pe_tf = [0]
            for cc in range(CC):
                @mk.sec("sync")
                def _(e, cc=cc):
                    if cc >= 2:
                        mk.w(e, "dv", dvk[cc - 2])
                    mk.dma(e, "qs", xc[cc % 2][:, :],
                           src[cc * 128:(cc + 1) * 128, t0:t0 + width])
                qsl = self.c["qs"]

                @mk.sec("vector")
                def _(e, cc=cc, qsl=qsl):
                    mk.w(e, "qs", qsl)
                    mk.se(e, e.tensor_tensor(tt[:, :], xc[cc % 2][:, :],
                                             sb["Rb"][:, t0:t0 + width], ALU.mult))
                    if extra is None:
                        i = e.tensor_tensor(xn[:, cc * width:(cc + 1) * width], tt[:, :],
                                            sb["MRb"][:, t0:t0 + width], ALU.subtract)
                    else:
                        mk.w(e, "pe", pe_tf[0])
                        mk.se(e, e.tensor_tensor(tf[:, :], tt[:, :],
                                                 sb["MRb"][:, t0:t0 + width], ALU.subtract))
                        i = e.tensor_copy(xn[:, cc * width:(cc + 1) * width], tf[:, :])
                    dvk[cc] = mk.inc("dv", i)
                if extra is not None:
                    pe_tf[0] = extra(cc, tf, dvk[cc])
            self.barrier()

    def attn_a(self):
        mk, sb, a, nc = self, self.sb, self.a, self.nc
        with ExitStack() as ph:
            xn = ph.enter_context(_sb(nc, "aa_xn", [128, CC * TL], BF16))
            self.emit_xn(xn, a["x_in"])

            wbuf = [ph.enter_context(_sb(nc, f"aa_w{i}", [128, 2 * CC * 128], BF16))
                    for i in range(2)]
            xc = [ph.enter_context(_sb(nc, f"aa_xc{i}", [128, TL], F32)) for i in range(2)]
            tmp1 = ph.enter_context(_sb(nc, "aa_t1", [128, TL], F32))
            tmp2 = ph.enter_context(_sb(nc, "aa_t2", [128, TL], F32))
            r_sb = ph.enter_context(_sb(nc, "aa_r", [128, TL], F32))
            i_sb = ph.enter_context(_sb(nc, "aa_i", [128, TL], F32))
            a_sb = ph.enter_context(_sb(nc, "aa_a", [128, TL], F32))
            sq_sb = ph.enter_context(_sb(nc, "aa_sq", [128, TL], F32))
            u_sb = ph.enter_context(_sb(nc, "aa_u", [128, TL], F32))
            y_sb = ph.enter_context(_sb(nc, "aa_y", [128, TL], F32))
            st32 = ph.enter_context(_sb(nc, "aa_st", [128, 32], F32))
            gt = [ph.enter_context(_sb(nc, f"aa_g{i}", [128, 32], F32)) for i in range(4)]
            P1 = ph.enter_context(_sb(nc, "aa_P1", [128, 16], F32))
            P2 = ph.enter_context(_sb(nc, "aa_P2", [128, 16], F32))
            hin = ph.enter_context(_sb(nc, "aa_hin", [128, 16], F32))
            selb = ph.enter_context(_sb(nc, "aa_selb", [128, 4], F32))
            ps_r = ph.enter_context(_ps(nc, "aa_psr", [128, TL], F32))
            ps_i = ph.enter_context(_ps(nc, "aa_psi", [128, TL], F32))

            wld = {}; pemm = {}; acsig = {}; aca = {}; dvloga = {}; dvu = {}
            dvscan = {}; qsx = {}; spq = {}

            for step in range(CC + 1):
                if step < CC:
                    @mk.sec("gpsimd")
                    def _(e, hh=step):
                        if hh >= 2:
                            mk.w(e, "pe", pemm[hh - 2])
                        for mi, wn in [(0, "WaT"), (1, "WxT")]:
                            src = a[wn][:, hh * 128:(hh + 1) * 128]
                            src = src.rearrange("(cc p) h -> p cc h", p=128)
                            dst = wbuf[hh % 2][:, mi * CC * 128:(mi + 1) * CC * 128]
                            dst = dst.rearrange("p (cc h) -> p cc h", h=128)
                            mk.dma(e, "qg", dst, src)
                        wld[hh] = self.c["qg"]
                if step == 0:
                    continue
                h = step - 1

                @mk.sec("tensor")
                def _(e, h=h):
                    mk.w(e, "qg", wld[h])
                    if h >= 1:
                        mk.w(e, "ac", acsig[h - 1])
                    for mi, ps in [(0, ps_r), (1, ps_i)]:
                        for cc in range(CC):
                            for nt in range(4):
                                i = e.matmul(
                                    ps[:, nt * 512:(nt + 1) * 512],
                                    wbuf[h % 2][:, (mi * CC + cc) * 128:(mi * CC + cc + 1) * 128],
                                    xn[:, cc * TL + nt * 512: cc * TL + (nt + 1) * 512],
                                    start=(cc == 0), stop=(cc == CC - 1),
                                    skip_group_check=True)
                    pemm[h] = mk.inc("pe", i)

                @mk.sec("scalar")
                def _(e, h=h):
                    mk.w(e, "pe", pemm[h])
                    if h >= 1:
                        mk.w(e, "dv", dvu[h - 1])
                    e.activation(out=r_sb[:, :], in_=ps_r[:, :], func=AF.Sigmoid,
                                 bias=sb["ba"][:, h:h + 1])
                    i = e.activation(out=i_sb[:, :], in_=ps_i[:, :], func=AF.Sigmoid,
                                     bias=sb["bx"][:, h:h + 1])
                    acsig[h] = mk.inc("ac", i)

                @mk.sec("vector")
                def _(e, h=h):
                    mk.w(e, "ac", acsig[h])
                    i = e.tensor_scalar(tmp1[:, :], r_sb[:, :], sb["ngam"][:, h:h + 1],
                                        None, ALU.mult)
                    dvloga[h] = mk.inc("dv", i)

                @mk.sec("scalar")
                def _(e, h=h):
                    mk.w(e, "dv", dvloga[h])
                    if h >= 1:
                        mk.w(e, "qs", spq[h - 1])
                    mk.se(e, e.activation(out=a_sb[:, :], in_=tmp1[:, :], func=AF.Exp), "ac")
                    mk.se(e, e.square(tmp1[:, :], a_sb[:, :]), "ac")
                    i = e.activation(out=sq_sb[:, :], in_=tmp1[:, :], func=AF.Sqrt,
                                     scale=sb["constsA"][:, 2:3], bias=sb["constsA"][:, 0:1])
                    aca[h] = mk.inc("ac", i)

                @mk.sec("sync")
                def _(e, h=h):
                    if h >= 2:
                        mk.w(e, "dv", dvu[h - 2])
                    qsx[h] = mk.dma(e, "qs", xc[h % 2][:, :],
                                    a["x_in"][h * 128:(h + 1) * 128, :])

                @mk.sec("vector")
                def _(e, h=h):
                    mk.w(e, "qs", qsx[h])
                    mk.w(e, "ac", aca[h])
                    if h >= 1:
                        mk.w(e, "qs", spq[h - 1])
                    mk.se(e, e.tensor_tensor(tmp2[:, :], xc[h % 2][:, :], sb["Rb"][:, :],
                                             ALU.mult))
                    mk.se(e, e.tensor_tensor(tmp2[:, :], tmp2[:, :], sb["MRb"][:, :],
                                             ALU.subtract))
                    mk.se(e, e.tensor_tensor(tmp2[:, :], i_sb[:, :], tmp2[:, :], ALU.mult))
                    i = e.tensor_tensor(u_sb[:, :], tmp2[:, :], sq_sb[:, :], ALU.mult)
                    dvu[h] = mk.inc("dv", i)
                    mk.w(e, "dv", dvu[h])
                    e.tensor_reduce(st32[:, h:h + 1], a_sb[:, :], AX.X, ALU.mult)
                    i = e.tensor_tensor_scan(y_sb[:, :], a_sb[:, :], u_sb[:, :], 0.0,
                                             ALU.mult, ALU.add)
                    mk.w(e, "dv", mk.inc("dv", i))
                    i = e.tensor_copy(st32[:, 16 + h:17 + h], y_sb[:, TL - 1:TL])
                    dvscan[h] = mk.inc("dv", i)

                @mk.sec("sync")
                def _(e, h=h):
                    mk.w(e, "dv", dvscan[h])
                    mk.dma(e, "qs", a["sp_a"][h * 128:(h + 1) * 128, :], a_sb[:, :])
                    spq[h] = mk.dma(e, "qs", a["sp_u"][h * 128:(h + 1) * 128, :], u_sb[:, :])
                    if DEBUG_AA[0] is not False and h == DEBUG_AA[0]:
                        mk.dma(e, "qs", a["dbg_a"][:, :], a_sb[:, :])
                        mk.dma(e, "qs", a["dbg_u"][:, :], u_sb[:, :])
                        mk.dma(e, "qs", a["dbg_r"][:, :], i_sb[:, :])
                        mk.dma(e, "qs", a["dbg_sq"][:, :], sq_sb[:, :])
                        spq[h] = mk.dma(e, "qs", a["dbg_y"][:, :], y_sb[:, :])

            @mk.sec("sync")
            def _(e):
                mk.w(e, "dv", dvscan[CC - 1])
                mk.dma(e, "qs", a["cc_scan_in"][:, :], st32[:, :])
            qs_sc = self.c["qs"]

            @mk.sec("gpsimd")
            def _(e):
                mk.w(e, "qs", qs_sc)
                i = e.collective_compute(
                    "AllGather", ALU.bypass, replica_groups=GROUPS,
                    ins=[a["cc_scan_in"].opt()], outs=[a["cc_scan_out"].opt()])
                mk.inc("cc", i)
            cc_sc = self.c["cc"]

            @mk.sec("sync")
            def _(e):
                mk.w(e, "cc", cc_sc)
                for r in range(4):
                    mk.dma(e, "qs", gt[r][:, :], a["cc_scan_out"][r * 128:(r + 1) * 128, :])
            qs_gt = self.c["qs"]

            @mk.sec("tensor")
            def _(e):
                mk.w(e, "ac", acsig[CC - 1])
                i = e.matmul(ps_r[:, 0:4], sb["ones_row_f"][:, :], sb["sel4"][:, :],
                             start=True, stop=True)
                mk.inc("pe", i)
            pe_sel = self.c["pe"]

            @mk.sec("vector")
            def _(e):
                mk.w(e, "pe", pe_sel)
                mk.w(e, "qs", qs_gt)
                mk.se(e, e.tensor_copy(selb[:, :], ps_r[:, 0:4]))
                mk.se(e, e.tensor_tensor(P1[:, :], gt[1][:, 0:16], gt[0][:, 16:32], ALU.mult))
                mk.se(e, e.tensor_tensor(P1[:, :], P1[:, :], gt[1][:, 16:32], ALU.add))
                mk.se(e, e.tensor_tensor(P2[:, :], gt[2][:, 0:16], P1[:, :], ALU.mult))
                mk.se(e, e.tensor_tensor(P2[:, :], P2[:, :], gt[2][:, 16:32], ALU.add))
                mk.se(e, e.tensor_scalar(hin[:, :], gt[0][:, 16:32], selb[:, 0:1], None,
                                         ALU.mult))
                mk.se(e, e.tensor_scalar(gt[3][:, 0:16], P1[:, :], selb[:, 1:2], None,
                                         ALU.mult))
                mk.se(e, e.tensor_tensor(hin[:, :], hin[:, :], gt[3][:, 0:16], ALU.add))
                mk.se(e, e.tensor_scalar(gt[3][:, 0:16], P2[:, :], selb[:, 2:3], None,
                                         ALU.mult))
                i = e.tensor_tensor(hin[:, :], hin[:, :], gt[3][:, 0:16], ALU.add)
                mk.inc("dv", i)
            dv_hin = self.c["dv"]

            if DEBUG_AA[0]:
                @mk.sec("sync")
                def _(e):
                    mk.w(e, "dv", dv_hin)
                    mk.dma(e, "qs", a["dbg_st"][:, :], st32[:, :])
                    mk.dma(e, "qs", a["dbg_hin"][:, :], hin[:, :])
                    mk.dma(e, "qs", a["dbg_selb"][:, :], selb[:, :])

            ldq = {}; dvs2 = {}; stq = {}
            bufA = [(a_sb, u_sb), (r_sb, i_sb)]
            for h in range(CC):
                @mk.sec("sync")
                def _(e, h=h):
                    if h == 0:
                        mk.w(e, "dv", dv_hin)
                        mk.w(e, "qs", spq[CC - 1])
                    if h >= 2:
                        mk.w(e, "dv", dvs2[h - 2])
                    A, U = bufA[h % 2]
                    mk.dma(e, "qs", A[:, :], a["sp_a"][h * 128:(h + 1) * 128, :])
                    mk.dma(e, "qs", U[:, :], a["sp_u"][h * 128:(h + 1) * 128, :])
                    ldq[h] = mk.dma(e, "qs", xc[h % 2][:, :],
                                    a["x_in"][h * 128:(h + 1) * 128, :])

                @mk.sec("vector")
                def _(e, h=h):
                    mk.w(e, "qs", ldq[h])
                    if h >= 1:
                        mk.w(e, "qs", stq[h - 1])
                    A, U = bufA[h % 2]
                    mk.se(e, e.tensor_tensor_scan(y_sb[:, :], A[:, :], U[:, :],
                                                  hin[:, h:h + 1], ALU.mult, ALU.add))
                    i = e.tensor_tensor(y_sb[:, :], y_sb[:, :], xc[h % 2][:, :], ALU.add)
                    dvs2[h] = mk.inc("dv", i)

                @mk.sec("sync")
                def _(e, h=h):
                    mk.w(e, "dv", dvs2[h])
                    stq[h] = mk.dma(e, "qs", a["x_fm"][h * 128:(h + 1) * 128, :], y_sb[:, :])
            self.barrier()

    def ffn(self, dst_name):
        mk, sb, a, nc = self, self.sb, self.a, self.nc
        with ExitStack() as ph:
            xf = ph.enter_context(_sb(nc, "ff_xf", [128, CC * TH], BF16))
            y_sb = ph.enter_context(_sb(nc, "ff_y", [128, CC * TH], F32))
            h_e = ph.enter_context(_sb(nc, "ff_h", [128, FCH * TH], BF16))
            B_sb = ph.enter_context(_sb(nc, "ff_B", [128, TH], F32))
            cmb = ph.enter_context(_sb(nc, "ff_cmb", [9, TH], F32))
            rt_sb = ph.enter_context(_sb(nc, "ff_rt", [9, TH], F32))
            lgt = ph.enter_context(_sb(nc, "ff_lgt", [128, 9], F32))
            msk = ph.enter_context(_sb(nc, "ff_msk", [128, 16], F32))
            srow = ph.enter_context(_sb(nc, "ff_srow", [128, 2], F32))
            w1 = [ph.enter_context(_sb(nc, f"ff_w1{i}", [128, CC * 128], BF16))
                  for i in range(2)]
            w2 = [ph.enter_context(_sb(nc, f"ff_w2{i}", [128, CC * 128], BF16))
                  for i in range(2)]
            wp = [ph.enter_context(_sb(nc, f"ff_wp{i}", [128, FCH * 256], BF16))
                  for i in range(2)]
            sil = ph.enter_context(_sb(nc, "ff_sil", [128, TH], BF16))
            h2s = ph.enter_context(_sb(nc, "ff_h2s", [128, TH], BF16))
            xc = [ph.enter_context(_sb(nc, f"ff_xc{i}", [128, TH], F32)) for i in range(2)]
            P = [ph.enter_context(_ps(nc, f"ff_P{i}", [128, TH], F32)) for i in range(4)]

            st_half = {-1: 0, 0: 0}
            for hp in range(2):
                t0 = hp * TH
                rtk = [0]

                def router_cb(cc, tf, dvt, rtk=rtk):
                    @mk.sec("tensor")
                    def _(e, cc=cc):
                        mk.w(e, "dv", dvt)
                        if cc == 0:
                            mk.w(e, "dv", self.c["dv"])
                        for nt in range(2):
                            i = e.matmul(P[0][0:9, nt * 512:(nt + 1) * 512],
                                         sb["gate9"][:, cc * 9:(cc + 1) * 9],
                                         tf[:, nt * 512:(nt + 1) * 512],
                                         start=(cc == 0), stop=(cc == CC - 1),
                                         skip_group_check=True)
                        rtk[0] = mk.inc("pe", i)
                    return rtk[0]

                self.emit_xn(xf, a["x_fm"], width=TH, t0=t0, extra=router_cb, xcb=xc)

                @mk.sec("vector")
                def _(e):
                    mk.w(e, "pe", rtk[0])
                    i = e.tensor_copy(rt_sb[:, :], P[0][0:9, :])
                    mk.inc("dv", i)

                for tt in range(8):
                    @mk.sec("tensor")
                    def _(e, tt=tt):
                        mk.w(e, "dv", self.c["dv"])
                        i = e.transpose(P[1][:, 0:9], rt_sb[0:9, tt * 128:(tt + 1) * 128],
                                        sb["id_f"][0:9, 0:9])
                        mk.inc("pe", i)

                    @mk.sec("vector")
                    def _(e, tt=tt):
                        mk.w(e, "pe", self.c["pe"])
                        mk.se(e, e.tensor_copy(lgt[:, :], P[1][:, 0:9]))
                        mk.se(e, e.reduce_max(srow[:, 0:1], lgt[:, 0:8], AX.X))
                        i = e.tensor_scalar(srow[:, 1:2], srow[:, 0:1], -1.0, None, ALU.mult)
                        mk.inc("dv", i)

                    @mk.sec("scalar")
                    def _(e, tt=tt):
                        mk.w(e, "dv", self.c["dv"])
                        e.activation(out=lgt[:, 0:8], in_=lgt[:, 0:8], func=AF.Exp,
                                     bias=srow[:, 1:2], accum_out=srow[:, 0:1])
                        i = e.activation(out=lgt[:, 8:9], in_=lgt[:, 8:9], func=AF.Sigmoid)
                        mk.inc("ac", i)

                    @mk.sec("vector")
                    def _(e, tt=tt):
                        mk.w(e, "ac", self.c["ac"])
                        mk.se(e, e.reciprocal(srow[:, 1:2], srow[:, 0:1]))
                        mk.se(e, e.tensor_scalar(lgt[:, 0:8], lgt[:, 0:8], srow[:, 1:2], None,
                                                 ALU.mult))
                        mk.se(e, e.reduce_max(srow[:, 0:1], lgt[:, 0:8], AX.X))
                        mk.se(e, e.tensor_scalar(msk[:, 0:8], lgt[:, 0:8], srow[:, 0:1], None,
                                                 ALU.is_ge))
                        mk.se(e, e.scalar_tensor_tensor(msk[:, 8:16], msk[:, 0:8], -1e30,
                                                        lgt[:, 0:8], ALU.mult, ALU.add))
                        mk.se(e, e.reduce_max(srow[:, 1:2], msk[:, 8:16], AX.X))
                        mk.se(e, e.tensor_scalar(msk[:, 0:8], lgt[:, 0:8], srow[:, 1:2], None,
                                                 ALU.is_ge))
                        i = e.tensor_tensor(lgt[:, 0:8], lgt[:, 0:8], msk[:, 0:8], ALU.mult)
                        mk.inc("dv", i)

                    @mk.sec("tensor")
                    def _(e, tt=tt):
                        mk.w(e, "dv", self.c["dv"])
                        i = e.transpose(P[2][0:9, 0:128], lgt[:, 0:9], sb["id_f"][:, :])
                        mk.inc("pe", i)

                    @mk.sec("vector")
                    def _(e, tt=tt):
                        mk.w(e, "pe", self.c["pe"])
                        i = e.tensor_copy(cmb[:, tt * 128:(tt + 1) * 128], P[2][0:9, 0:128])
                        mk.inc("dv", i)

                pefc = {}; acsil = {}; dvh = {}
                pepj = {}; dvy = {}
                dvB = 0
                for ex in range(NEXP):
                    @mk.sec("tensor")
                    def _(e, ex=ex):
                        mk.w(e, "dv", self.c["dv"])
                        ei = ex if ex < 8 else 8
                        for nt in range(2):
                            i = e.matmul(P[3][:, nt * 512:(nt + 1) * 512],
                                         sb["eind"][:, ei * 128:(ei + 1) * 128],
                                         cmb[0:9, nt * 512:(nt + 1) * 512],
                                         start=True, stop=True, skip_group_check=True)
                        mk.inc("pe", i)

                    @mk.sec("vector")
                    def _(e, ex=ex):
                        mk.w(e, "pe", self.c["pe"])
                        i = e.tensor_copy(B_sb[:, :], P[3][:, :])
                        mk.se(e, i)
                        mk.inc("dv", e.engine_nop())
                    dvB = self.c["dv"]

                    for fc in range(FCH):
                        k = ex * FCH + fc

                        @mk.sec("gpsimd")
                        def _(e, ex=ex, fc=fc, k=k):
                            if k >= 2:
                                mk.w(e, "pe", pefc[k - 2])
                            for wn, dstb in [("fc1", w1[fc % 2]), ("fc2", w2[fc % 2])]:
                                if ex < 8:
                                    src = a[wn + "T"][ex, :, fc * 128:(fc + 1) * 128]
                                else:
                                    c0 = (ex - 8) * FM + fc * 128
                                    src = a["s" + wn + "T"][:, c0:c0 + 128]
                                src = src.rearrange("(cc p) f -> p cc f", p=128)
                                dst = dstb[:, :].rearrange("p (cc f) -> p cc f", f=128)
                                mk.dma(e, "qg", dst, src)
                        wldk = self.c["qg"]

                        @mk.sec("tensor")
                        def _(e, ex=ex, fc=fc, k=k, wldk=wldk, dvB=dvB):
                            mk.w(e, "qg", wldk)
                            if k >= 2:
                                mk.w(e, "ac", acsil[k - 2])
                                mk.w(e, "dv", dvh[k - 2])
                            else:
                                mk.w(e, "dv", dvB)
                            pA = P[0] if fc % 2 == 0 else P[2]
                            pB = P[1] if fc % 2 == 0 else P[3]
                            for wb, ps in [(w1[fc % 2], pA), (w2[fc % 2], pB)]:
                                for cc in range(CC):
                                    for nt in range(2):
                                        i = e.matmul(
                                            ps[:, nt * 512:(nt + 1) * 512],
                                            wb[:, cc * 128:(cc + 1) * 128],
                                            xf[:, cc * TH + nt * 512: cc * TH + (nt + 1) * 512],
                                            start=(cc == 0), stop=(cc == CC - 1),
                                            skip_group_check=True)
                            pefc[k] = mk.inc("pe", i)

                        @mk.sec("scalar")
                        def _(e, fc=fc, k=k):
                            mk.w(e, "pe", pefc[k])
                            if k >= 1:
                                mk.w(e, "dv", dvh[k - 1])
                            pA = P[0] if fc % 2 == 0 else P[2]
                            i = e.activation(out=sil[:, :], in_=pA[:, :], func=AF.Silu)
                            acsil[k] = mk.inc("ac", i)

                        @mk.sec("vector")
                        def _(e, fc=fc, k=k, dvB=dvB):
                            mk.w(e, "pe", pefc[k])
                            pB = P[1] if fc % 2 == 0 else P[3]
                            mk.se(e, e.tensor_tensor(h2s[:, :], pB[:, :], B_sb[:, :], ALU.mult))
                            mk.w(e, "ac", acsil[k])
                            i = e.tensor_tensor(h_e[:, fc * TH:(fc + 1) * TH], sil[:, :],
                                                h2s[:, :], ALU.mult)
                            dvh[k] = mk.inc("dv", i)

                    for q in range(8):
                        kq = ex * 8 + q

                        @mk.sec("gpsimd")
                        def _(e, ex=ex, q=q, kq=kq):
                            if kq >= 2:
                                mk.w(e, "pe", pepj[kq - 2])
                            if ex < 8:
                                src = a["projT"][ex, :, q * 256:(q + 1) * 256]
                            else:
                                src = a["sprojT"][(ex - 8) * FM:(ex - 7) * FM,
                                                  q * 256:(q + 1) * 256]
                            src = src.rearrange("(fc p) c -> p fc c", p=128)
                            dst = wp[q % 2][:, :].rearrange("p (fc c) -> p fc c", c=256)
                            mk.dma(e, "qg", dst, src)
                        wldq = self.c["qg"]

                        for cc4 in range(2):
                            ccg = q * 2 + cc4

                            @mk.sec("tensor")
                            def _(e, ex=ex, q=q, kq=kq, cc4=cc4, wldq=wldq):
                                mk.w(e, "qg", wldq)
                                mk.w(e, "dv", dvh[ex * FCH + FCH - 1])
                                ps = P[0] if cc4 % 2 == 0 else P[1]
                                war = dvy.get((ex, q - 1, cc4))
                                if war is None and ex > 0:
                                    war = dvy.get((ex - 1, 7, cc4))
                                if war:
                                    mk.w(e, "dv", war)
                                for fc in range(FCH):
                                    for nt in range(2):
                                        i = e.matmul(
                                            ps[:, nt * 512:(nt + 1) * 512],
                                            wp[q % 2][:, fc * 256 + cc4 * 128:
                                                      fc * 256 + (cc4 + 1) * 128],
                                            h_e[:, fc * TH + nt * 512: fc * TH + (nt + 1) * 512],
                                            start=(fc == 0), stop=(fc == FCH - 1),
                                            skip_group_check=True)
                                pepj[kq] = mk.inc("pe", i)

                            @mk.sec("vector")
                            def _(e, ex=ex, q=q, cc4=cc4, ccg=ccg, hp=hp):
                                mk.w(e, "pe", self.c["pe"])
                                ps = P[0] if cc4 % 2 == 0 else P[1]
                                if ex == 0:
                                    mk.w(e, "qs", st_half[hp - 1])
                                    i = e.tensor_copy(y_sb[:, ccg * TH:(ccg + 1) * TH], ps[:, :])
                                else:
                                    i = e.tensor_tensor(y_sb[:, ccg * TH:(ccg + 1) * TH],
                                                        y_sb[:, ccg * TH:(ccg + 1) * TH],
                                                        ps[:, :], ALU.add)
                                dvy[(ex, q, cc4)] = mk.inc("dv", i)

                dvres = {}
                for cc in range(CC):
                    @mk.sec("sync")
                    def _(e, cc=cc):
                        if cc >= 2:
                            mk.w(e, "dv", dvres[cc - 2])
                        mk.dma(e, "qs", xc[cc % 2][:, :],
                               a["x_fm"][cc * 128:(cc + 1) * 128, t0:t0 + TH])
                    qres = self.c["qs"]

                    @mk.sec("vector")
                    def _(e, cc=cc, qres=qres):
                        mk.w(e, "qs", qres)
                        i = e.tensor_tensor(y_sb[:, cc * TH:(cc + 1) * TH],
                                            y_sb[:, cc * TH:(cc + 1) * TH],
                                            xc[cc % 2][:, :], ALU.add)
                        dvres[cc] = mk.inc("dv", i)

                    @mk.sec("sync")
                    def _(e, cc=cc):
                        mk.w(e, "dv", dvres[cc])
                        mk.dma(e, "qs", a[dst_name][cc * 128:(cc + 1) * 128, t0:t0 + TH],
                               y_sb[:, cc * TH:(cc + 1) * TH])
                st_half[hp] = self.c["qs"]
            self.barrier()

    def attn_b(self):
        mk, sb, a, nc = self, self.sb, self.a, self.nc
        with ExitStack() as ph:
            q_bf = ph.enter_context(_sb(nc, "ab_q", [128, CC * TL], BF16))
            bvb = ph.enter_context(_sb(nc, "ab_bvb", [128, D], F32))
            negB = ph.enter_context(_sb(nc, "ab_negB", [128, 16], F32))
            qn = ph.enter_context(_sb(nc, "ab_qn", [1, 16], F32))
            kn = ph.enter_context(_sb(nc, "ab_kn", [1, 16], F32))
            kng = [ph.enter_context(_sb(nc, f"ab_kng{i}", [1, 16], F32)) for i in range(4)]

            with ExitStack() as s0:
                ps_b = s0.enter_context(_ps(nc, "ab_psb", [128, 512], F32))
                bvr = s0.enter_context(_sb(nc, "ab_bvr", [1, D], F32))

                @mk.sec("sync")
                def _(e):
                    mk.dma(e, "qs", bvr[:, :], a["bv_row"][:, :])
                nl = self.c["qs"]
                for nt in range(4):
                    @mk.sec("tensor")
                    def _(e, nt=nt, nl=nl):
                        if nt == 0:
                            mk.w(e, "qs", nl)
                        else:
                            mk.w(e, "dv", self.c["dv"])
                        i = e.matmul(ps_b[:, :], sb["ones_row_f"][:, :],
                                     bvr[:, nt * 512:(nt + 1) * 512], start=True, stop=True)
                        mk.inc("pe", i)

                    @mk.sec("vector")
                    def _(e, nt=nt):
                        mk.w(e, "pe", self.c["pe"])
                        i = e.tensor_copy(bvb[:, nt * 512:(nt + 1) * 512], ps_b[:, :])
                        mk.inc("dv", i)
                self.barrier()

            with ExitStack() as s1:
                xn = s1.enter_context(_sb(nc, "ab_xn", [128, CC * TL], BF16))
                self.emit_xn(xn, a["x_fm"])

                wbuf = [s1.enter_context(_sb(nc, f"ab_w{i}", [128, CC * 128], BF16))
                        for i in range(2)]
                kev = [s1.enter_context(_sb(nc, f"ab_kev{i}", [128, TL], BF16))
                       for i in range(2)]
                ksq = s1.enter_context(_sb(nc, "ab_ksq", [128, TL], BF16))
                wv = s1.enter_context(_sb(nc, "ab_wv", [128, CC * 512], BF16))

                def proj_loop(wname, brow, scale, store, qdst, nrow):
                    wld = {}; pemm = {}; acev = {}; dvnm = {}; stq = {}
                    with ExitStack() as s2:
                        ps_k = s2.enter_context(_ps(nc, "ab_psk", [128, TL], F32))
                        ps_n = s2.enter_context(_ps(nc, "ab_psn", [1, TL], F32))
                        for step in range(CC + 1):
                            if step < CC:
                                @mk.sec("gpsimd")
                                def _(e, hh=step):
                                    if hh >= 2:
                                        mk.w(e, "pe", pemm[hh - 2])
                                    src = a[wname][:, hh * 128:(hh + 1) * 128]
                                    src = src.rearrange("(cc p) h -> p cc h", p=128)
                                    dst = wbuf[hh % 2][:, :].rearrange("p (cc h) -> p cc h",
                                                                       h=128)
                                    mk.dma(e, "qg", dst, src)
                                    wld[hh] = self.c["qg"]
                            if step == 0:
                                continue
                            h = step - 1

                            @mk.sec("tensor")
                            def _(e, h=h):
                                mk.w(e, "qg", wld[h])
                                if h >= 1:
                                    mk.w(e, "ac", acev[h - 1])
                                for cc in range(CC):
                                    for nt in range(4):
                                        i = e.matmul(
                                            ps_k[:, nt * 512:(nt + 1) * 512],
                                            wbuf[h % 2][:, cc * 128:(cc + 1) * 128],
                                            xn[:, cc * TL + nt * 512: cc * TL + (nt + 1) * 512],
                                            start=(cc == 0), stop=(cc == CC - 1),
                                            skip_group_check=True)
                                pemm[h] = mk.inc("pe", i)

                            @mk.sec("scalar")
                            def _(e, h=h):
                                mk.w(e, "pe", pemm[h])
                                if h >= 2 and store:
                                    mk.w(e, "qs", stq[h - 2])
                                if h >= 1:
                                    mk.w(e, "dv", dvnm[h - 1])
                                dst = (kev[h % 2][:, :] if store
                                       else q_bf[:, h * TL:(h + 1) * TL])
                                if scale is None:
                                    i = e.activation(out=dst, in_=ps_k[:, :], func=AF.Identity,
                                                     bias=brow[:, h:h + 1])
                                else:
                                    i = e.activation(out=dst, in_=ps_k[:, :], func=AF.Identity,
                                                     scale=scale, bias=brow[:, h:h + 1])
                                mk.se(e, i, "ac")
                                i = e.square(ksq[:, :], dst)
                                acev[h] = mk.inc("ac", i)

                            @mk.sec("tensor")
                            def _(e, h=h):
                                mk.w(e, "ac", acev[h])
                                if h >= 1:
                                    mk.w(e, "dv", dvnm[h - 1])
                                for nt in range(4):
                                    i = e.matmul(ps_n[0:1, nt * 512:(nt + 1) * 512],
                                                 sb["ones_col_b"][:, :],
                                                 ksq[:, nt * 512:(nt + 1) * 512],
                                                 start=True, stop=True, skip_group_check=True)
                                mk.inc("pe", i)

                            @mk.sec("vector")
                            def _(e, h=h):
                                mk.w(e, "pe", self.c["pe"])
                                i = e.reduce_max(nrow[0:1, h:h + 1], ps_n[0:1, :], AX.X)
                                dvnm[h] = mk.inc("dv", i)

                            if store:
                                @mk.sec("sync")
                                def _(e, h=h):
                                    mk.w(e, "ac", acev[h])
                                    stq[h] = mk.dma(e, "qs",
                                                    a["cc_kv_in"][h * 128:(h + 1) * 128, :],
                                                    kev[h % 2][:, :])
                        self.barrier()

                proj_loop("wkT", sb["bk"], None, True, None, kn)

                @mk.sec("sync")
                def _(e):
                    mk.dma(e, "qs", a["cc_kn_in"][:, :], kn[:, :])
                qs_kn = self.c["qs"]

                @mk.sec("gpsimd")
                def _(e):
                    mk.w(e, "qs", qs_kn)
                    i = e.collective_compute(
                        "AllGather", ALU.bypass, replica_groups=GROUPS,
                        ins=[a["cc_kn_in"].opt()], outs=[a["cc_kn_out"].opt()])
                    mk.inc("cc", i)
                cc_kn = self.c["cc"]

                with ExitStack() as s2:
                    ps_v = [s2.enter_context(_ps(nc, f"ab_psv{i}", [128, 512], F32))
                            for i in range(2)]
                    wvld = {}; pev = {}; dvv = {}; stv = {}
                    for chb in range(4):
                        @mk.sec("gpsimd")
                        def _(e, chb=chb):
                            if chb >= 1:
                                mk.w(e, "pe", pev[(chb - 1, CC - 1)])
                            src = a["wvT"][:, chb * 512:(chb + 1) * 512]
                            src = src.rearrange("(cc p) h -> p cc h", p=128)
                            dst = wv[:, :].rearrange("p (cc h) -> p cc h", h=512)
                            mk.dma(e, "qg", dst, src)
                        wvld[chb] = self.c["qg"]

                        for tt in range(CC):
                            @mk.sec("tensor")
                            def _(e, chb=chb, tt=tt):
                                mk.w(e, "qg", wvld[chb])
                                war = dvv.get((chb, tt - 2)) or (dvv.get((chb - 1, CC - 2 + tt))
                                                                 if chb >= 1 else 0)
                                if war:
                                    mk.w(e, "dv", war)
                                for cc in range(CC):
                                    i = e.matmul(
                                        ps_v[tt % 2][:, :],
                                        xn[:, cc * TL + tt * 128: cc * TL + (tt + 1) * 128],
                                        wv[:, cc * 512:(cc + 1) * 512],
                                        start=(cc == 0), stop=(cc == CC - 1),
                                        skip_group_check=True)
                                pev[(chb, tt)] = mk.inc("pe", i)

                            @mk.sec("vector")
                            def _(e, chb=chb, tt=tt):
                                mk.w(e, "pe", pev[(chb, tt)])
                                if stv.get((chb, tt - 2)) or (chb >= 1):
                                    mk.w(e, "qs", stv.get((chb, tt - 2),
                                                          stv.get((chb - 1, CC - 2 + tt), 0)))
                                i = e.tensor_tensor(kev[tt % 2][:, 0:512], ps_v[tt % 2][:, :],
                                                    bvb[:, chb * 512:(chb + 1) * 512],
                                                    ALU.add)
                                dvv[(chb, tt)] = mk.inc("dv", i)

                            @mk.sec("sync")
                            def _(e, chb=chb, tt=tt):
                                mk.w(e, "dv", dvv[(chb, tt)])
                                stv[(chb, tt)] = mk.dma(
                                    e, "qs",
                                    a["cc_kv_in"][TL + tt * 128:TL + (tt + 1) * 128,
                                                  chb * 512:(chb + 1) * 512],
                                    kev[tt % 2][:, 0:512])
                    self.barrier()

                @mk.sec("gpsimd")
                def _(e):
                    i = e.collective_compute(
                        "AllGather", ALU.bypass, replica_groups=GROUPS,
                        ins=[a["cc_kv_in"].opt()], outs=[a["cc_kv_out"].opt()])
                    mk.inc("cc", i)
                cc_kv = self.c["cc"]

                proj_loop("wqT", sb["bqs"], sb["constsA"][:, 3:4], False, q_bf, qn)

                @mk.sec("sync")
                def _(e):
                    mk.w(e, "cc", cc_kn)
                    for r in range(4):
                        mk.dma(e, "qs", kng[r][:, :], a["cc_kn_out"][r:r + 1, :])
                qs_kg = self.c["qs"]

                @mk.sec("vector")
                def _(e):
                    mk.w(e, "qs", qs_kg)
                    mk.se(e, e.tensor_max(kng[0][:, :], kng[0][:, :], kng[1][:, :]))
                    mk.se(e, e.tensor_max(kng[2][:, :], kng[2][:, :], kng[3][:, :]))
                    i = e.tensor_max(kng[0][:, :], kng[0][:, :], kng[2][:, :])
                    mk.inc("dv", i)

                @mk.sec("scalar")
                def _(e):
                    mk.w(e, "dv", self.c["dv"])
                    mk.se(e, e.activation(out=kng[1][:, :], in_=kng[0][:, :], func=AF.Sqrt),
                          "ac")
                    i = e.activation(out=kng[2][:, :], in_=qn[:, :], func=AF.Sqrt)
                    mk.inc("ac", i)

                @mk.sec("vector")
                def _(e):
                    mk.w(e, "ac", self.c["ac"])
                    mk.se(e, e.tensor_tensor(kng[3][:, :], kng[1][:, :], kng[2][:, :],
                                             ALU.mult))
                    i = e.tensor_scalar(kng[3][:, :], kng[3][:, :], -1.0, None, ALU.mult)
                    mk.inc("dv", i)

                with ExitStack() as s2:
                    ps_nb = s2.enter_context(_ps(nc, "ab_psnb", [128, 16], F32))

                    @mk.sec("tensor")
                    def _(e):
                        mk.w(e, "dv", self.c["dv"])
                        i = e.matmul(ps_nb[:, :], sb["ones_row_f"][:, :], kng[3][:, :],
                                     start=True, stop=True)
                        mk.inc("pe", i)

                    @mk.sec("vector")
                    def _(e):
                        mk.w(e, "pe", self.c["pe"])
                        i = e.tensor_copy(negB[:, :], ps_nb[:, :])
                        mk.inc("dv", i)
                    self.barrier()

            s2o = ph.enter_context(ExitStack())
            att = s2o.enter_context(_sb(nc, "ab_att", [128, CC * TL], BF16))
            with ExitStack() as s2:
                K_sb = s2.enter_context(_sb(nc, "ab_K", [128, 4 * TL], BF16))
                V_sb = s2.enter_context(_sb(nc, "ab_V", [128, 4 * TL], BF16))
                QQ = 512
                RING = 3
                pT = [s2.enter_context(_sb(nc, f"ab_pT{i}", [128, QQ], BF16))
                      for i in range(RING)]
                rs_row = s2.enter_context(_sb(nc, "ab_rs", [1, QQ], F32))
                rs_b = s2.enter_context(_sb(nc, "ab_rsb", [128, QQ], F32))
                ps_s = [s2.enter_context(_ps(nc, f"ab_pss{i}", [128, QQ], F32))
                        for i in range(RING)]
                ps_sum = [s2.enter_context(_ps(nc, f"ab_pssum{i}", [1, QQ], F32))
                          for i in range(2)]
                ps_o = [s2.enter_context(_ps(nc, f"ab_pso{i}", [128, QQ], F32))
                        for i in range(2)]
                ps_rs = s2.enter_context(_ps(nc, "ab_psrs", [128, QQ], F32))

                last_pe = {-1: 0}
                dv_fin = {}
                it = 0
                for h in range(NH):
                    @mk.sec("sync")
                    def _(e, h=h):
                        if h == 0:
                            mk.w(e, "cc", cc_kv)
                        else:
                            mk.w(e, "pe", last_pe[h - 1])
                        for r in range(4):
                            mk.dma(e, "qs", K_sb[:, r * TL:(r + 1) * TL],
                                   a["cc_kv_out"][r * 2 * TL + h * 128:
                                                  r * 2 * TL + (h + 1) * 128, :])
                    qsk = self.c["qs"]

                    @mk.sec("scalar")
                    def _(e, h=h):
                        if h == 0:
                            mk.w(e, "cc", cc_kv)
                        else:
                            mk.w(e, "pe", last_pe[h - 1])
                        for r in range(4):
                            src = a["cc_kv_out"][r * 2 * TL + TL: (r + 1) * 2 * TL,
                                                 h * 128:(h + 1) * 128]
                            src = src.rearrange("(tb p) f -> p tb f", p=128)
                            dst = V_sb[:, r * TL:(r + 1) * TL]
                            dst = dst.rearrange("p (tb f) -> p tb f", f=128)
                            mk.dma(e, "qc", dst, src)
                    qcv = self.c["qc"]

                    for qq in range(4):
                        pe0 = self.c["pe"]
                        ac0 = self.c["ac"]
                        q0 = h * TL + qq * QQ

                        @mk.sec("tensor")
                        def _(e, h=h, qq=qq, it=it, pe0=pe0, ac0=ac0, qsk=qsk, qcv=qcv,
                              q0=q0):
                            if qq == 0:
                                mk.w(e, "qs", qsk)
                                mk.w(e, "qc", qcv)
                            mk.w(e, "dv", dv_fin.get(it - 2, 0))
                            for kb in range(64):
                                if kb >= RING:
                                    mk.w(e, "ac", ac0 + kb - RING + 1)
                                i = e.matmul(ps_s[kb % RING][:, :],
                                             K_sb[:, kb * 128:(kb + 1) * 128],
                                             q_bf[:, q0:q0 + QQ],
                                             start=True, stop=True,
                                             skip_group_check=True)
                                mk.inc("pe", i)
                                mk.w(e, "ac", ac0 + kb + 1)
                                i = e.matmul(ps_sum[it % 2][0:1, :],
                                             sb["ones_col_b"][:, :], pT[kb % RING][:, :],
                                             start=(kb == 0), stop=(kb == 63),
                                             skip_group_check=True)
                                mk.inc("pe", i)
                                i = e.matmul(ps_o[it % 2][:, :],
                                             V_sb[:, kb * 128:(kb + 1) * 128],
                                             pT[kb % RING][:, :],
                                             start=(kb == 0), stop=(kb == 63),
                                             skip_group_check=True)
                                mk.inc("pe", i)

                        @mk.sec("scalar")
                        def _(e, h=h, pe0=pe0):
                            for kb in range(64):
                                mk.w(e, "pe", pe0 + 3 * kb + 1)
                                i = e.activation(out=pT[kb % RING][:, :],
                                                 in_=ps_s[kb % RING][:, :],
                                                 func=AF.Exp, bias=negB[:, h:h + 1])
                                mk.inc("ac", i)

                        pe_done = self.c["pe"]

                        @mk.sec("vector")
                        def _(e, it=it, pe_done=pe_done):
                            mk.w(e, "pe", pe_done)
                            i = e.reciprocal(rs_row[:, :], ps_sum[it % 2][0:1, :])
                            mk.inc("dv", i)

                        @mk.sec("tensor")
                        def _(e):
                            mk.w(e, "dv", self.c["dv"])
                            i = e.matmul(ps_rs[:, :], sb["ones_row_f"][:, :],
                                         rs_row[:, :], start=True, stop=True,
                                         skip_group_check=True)
                            mk.inc("pe", i)
                        last_pe[h] = self.c["pe"]

                        @mk.sec("vector")
                        def _(e, it=it, q0=q0):
                            mk.w(e, "pe", self.c["pe"])
                            mk.se(e, e.tensor_copy(rs_b[:, :], ps_rs[:, :]))
                            i = e.tensor_tensor(att[:, q0:q0 + QQ],
                                                ps_o[it % 2][:, :], rs_b[:, :], ALU.mult)
                            dv_fin[it] = mk.inc("dv", i)
                        it += 1
                self.barrier()

            with ExitStack() as s3:
                wbuf2 = [s3.enter_context(_sb(nc, f"ab_wo{i}", [128, CC * 128], BF16))
                         for i in range(2)]
                o_sb = [s3.enter_context(_sb(nc, f"ab_o{i}", [128, TL], F32))
                        for i in range(2)]
                xc = [s3.enter_context(_sb(nc, f"ab_xc{i}", [128, TL], F32))
                      for i in range(2)]
                ps_op = [s3.enter_context(_ps(nc, f"ab_psop{i}", [128, TL], F32))
                         for i in range(2)]
                wld = {}; pemm = {}; acev = {}; dvres = {}; stq = {}; qx = {}
                for step in range(CC + 1):
                    if step < CC:
                        @mk.sec("gpsimd")
                        def _(e, co=step):
                            if co >= 2:
                                mk.w(e, "pe", pemm[co - 2])
                            src = a["woT"][:, co * 128:(co + 1) * 128]
                            src = src.rearrange("(cc p) h -> p cc h", p=128)
                            dst = wbuf2[co % 2][:, :].rearrange("p (cc h) -> p cc h", h=128)
                            mk.dma(e, "qg", dst, src)
                            wld[co] = self.c["qg"]

                        @mk.sec("sync")
                        def _(e, co=step):
                            if co >= 2:
                                mk.w(e, "dv", dvres[co - 2])
                            qx[co] = mk.dma(e, "qs", xc[co % 2][:, :],
                                            a["x_fm"][co * 128:(co + 1) * 128, :])
                    if step == 0:
                        continue
                    co = step - 1

                    @mk.sec("tensor")
                    def _(e, co=co):
                        mk.w(e, "qg", wld[co])
                        if co >= 2:
                            mk.w(e, "ac", acev[co - 2])
                        for cc in range(CC):
                            for nt in range(4):
                                i = e.matmul(
                                    ps_op[co % 2][:, nt * 512:(nt + 1) * 512],
                                    wbuf2[co % 2][:, cc * 128:(cc + 1) * 128],
                                    att[:, cc * TL + nt * 512: cc * TL + (nt + 1) * 512],
                                    start=(cc == 0), stop=(cc == CC - 1),
                                    skip_group_check=True)
                        pemm[co] = mk.inc("pe", i)

                    @mk.sec("scalar")
                    def _(e, co=co):
                        mk.w(e, "pe", pemm[co])
                        if co >= 2:
                            mk.w(e, "qs", stq[co - 2])
                        i = e.activation(out=o_sb[co % 2][:, :], in_=ps_op[co % 2][:, :],
                                         func=AF.Identity, bias=sb["bo"][:, co:co + 1])
                        acev[co] = mk.inc("ac", i)

                    @mk.sec("vector")
                    def _(e, co=co):
                        mk.w(e, "ac", acev[co])
                        mk.w(e, "qs", qx[co])
                        i = e.tensor_tensor(o_sb[co % 2][:, :], o_sb[co % 2][:, :],
                                            xc[co % 2][:, :], ALU.add)
                        dvres[co] = mk.inc("dv", i)

                    @mk.sec("sync")
                    def _(e, co=co):
                        mk.w(e, "dv", dvres[co])
                        stq[co] = mk.dma(e, "qs", a["x_fm"][co * 128:(co + 1) * 128, :],
                                         o_sb[co % 2][:, :])
                self.barrier()

    def finish(self):
        mk, a = self, self.a

        @mk.sec("sync")
        def _(e):
            for cc in range(CC):
                mk.dma(e, "qs", a["y_out"][cc * 128:(cc + 1) * 128, :],
                       a["x_fm"][cc * 128:(cc + 1) * 128, :])
        self.barrier()
        for en in ["sync", "gpsimd", "tensor", "vector", "scalar"]:
            @mk.sec(en)
            def _(e):
                pass


def build_program(upto=4):
    nc = bass.Bass("TRN2", target_bir_lowering=False, debug=False)

    def din(name, shape, dt=F32):
        return nc.dram_tensor(name, shape, dt, kind="ExternalInput").ap()

    aps = {
        "x_in": din("x_in", [D, TL]),
        "sel": din("sel", [1, 4]),
        "WaT": din("WaT", [D, D]),
        "WxT": din("WxT", [D, D]),
        "wqT": din("wqT", [D, D]),
        "wkT": din("wkT", [D, D]),
        "wvT": din("wvT", [D, D]),
        "woT": din("woT", [D, D]),
        "ba_c": din("ba_c", [128, 16]),
        "bx_c": din("bx_c", [128, 16]),
        "lam_c": din("lam_c", [128, 16]),
        "bq_c": din("bq_c", [128, 16]),
        "bk_c": din("bk_c", [128, 16]),
        "bo_c": din("bo_c", [128, 16]),
        "bv_row": din("bv_row", [1, D]),
        "fc1T": din("fc1T", [8, D, FM]),
        "fc2T": din("fc2T", [8, D, FM]),
        "projT": din("projT", [8, FM, D]),
        "sfc1T": din("sfc1T", [D, 4 * FM]),
        "sfc2T": din("sfc2T", [D, 4 * FM]),
        "sprojT": din("sprojT", [4 * FM, D]),
        "gate9T": din("gate9T", [128, CC * 9]),
        "ones_row_f": din("ones_row_f", [1, 128]),
        "constsA": din("constsA", [128, 8]),
        "ones_col_b": din("ones_col_b", [128, 1], BF16),
        "id_f": din("id_f", [128, 128]),
        "eind": din("eind", [9, 9 * 128]),
        "y_out": nc.dram_tensor("y_out", [D, TL], F32, kind="ExternalOutput").ap(),
        **({"dbg_a": nc.dram_tensor("dbg_a", [128, TL], F32, kind="ExternalOutput").ap(),
            "dbg_sq": nc.dram_tensor("dbg_sq", [128, TL], F32, kind="ExternalOutput").ap(),
            "dbg_y": nc.dram_tensor("dbg_y", [128, TL], F32, kind="ExternalOutput").ap(),
            "dbg_u": nc.dram_tensor("dbg_u", [128, TL], F32, kind="ExternalOutput").ap(),
            "dbg_r": nc.dram_tensor("dbg_r", [128, TL], F32, kind="ExternalOutput").ap(),
            "dbg_st": nc.dram_tensor("dbg_st", [128, 32], F32, kind="ExternalOutput").ap(),
            "dbg_hin": nc.dram_tensor("dbg_hin", [128, 16], F32, kind="ExternalOutput").ap(),
            "dbg_selb": nc.dram_tensor("dbg_selb", [128, 4], F32, kind="ExternalOutput").ap(),
            } if DEBUG_AA[0] is not False else {}),
        "x_fm": nc.dram_tensor("x_fm", [D, TL], F32).ap(),
        "sp_a": nc.dram_tensor("sp_a", [D, TL], F32).ap(),
        "sp_u": nc.dram_tensor("sp_u", [D, TL], F32).ap(),
        "cc_scan_in": nc.dram_tensor("cc_scan_in", [128, 32], F32).ap(),
        "cc_scan_out": nc.dram_tensor("cc_scan_out", [512, 32], F32).ap(),
        "cc_kn_in": nc.dram_tensor("cc_kn_in", [1, 16], F32).ap(),
        "cc_kn_out": nc.dram_tensor("cc_kn_out", [4, 16], F32).ap(),
        "cc_kv_in": nc.dram_tensor("cc_kv_in", [2 * TL, D], BF16).ap(),
        "cc_kv_out": nc.dram_tensor("cc_kv_out", [8 * TL, D], BF16).ap(),
    }

    with ExitStack() as es:
        block = es.enter_context(nc.Block())
        mk = MK(nc, es, block, aps)
        mk.alloc_persistent()
        mk.load_persistent()
        mk.stats_and_rows("x_in")
        mk.attn_a()
        if upto >= 2:
            mk.stats_and_rows("x_fm")
            mk.ffn("x_fm")
        if upto >= 3:
            mk.stats_and_rows("x_fm")
            mk.attn_b()
        if upto >= 4:
            mk.stats_and_rows("x_fm")
            mk.ffn("x_fm")
        mk.finish()
    return nc


def _prep_shared(i):
    f32 = np.float32
    ac = np.ascontiguousarray

    def chunk16(v):
        return ac(v.reshape(16, 128).T.astype(f32))

    sh = {
        "WaT": ac(np.asarray(i["L0_attn_a_Wa"]).T),
        "WxT": ac(np.asarray(i["L0_attn_a_Wx"]).T),
        "wqT": ac(np.asarray(i["L0_attn_b_wq"]).T),
        "wkT": ac(np.asarray(i["L0_attn_b_wk"]).T),
        "wvT": ac(np.asarray(i["L0_attn_b_wv"]).T),
        "woT": ac(np.asarray(i["L0_attn_b_wo"]).T),
        "ba_c": chunk16(np.asarray(i["L0_attn_a_ba"])),
        "bx_c": chunk16(np.asarray(i["L0_attn_a_bx"])),
        "lam_c": chunk16(np.asarray(i["L0_attn_a_Lam"])),
        "bq_c": chunk16(np.asarray(i["L0_attn_b_bq"])),
        "bk_c": chunk16(np.asarray(i["L0_attn_b_bk"])),
        "bo_c": chunk16(np.asarray(i["L0_attn_b_bo"])),
        "bv_row": ac(np.asarray(i["L0_attn_b_bv"]).reshape(1, D).astype(f32)),
        "fc1T": ac(np.asarray(i["L0_ffn_expert_fc1"]).transpose(0, 2, 1)),
        "fc2T": ac(np.asarray(i["L0_ffn_expert_fc2"]).transpose(0, 2, 1)),
        "projT": ac(np.asarray(i["L0_ffn_expert_proj"]).transpose(0, 2, 1)),
        "sfc1T": ac(np.asarray(i["L0_ffn_shared_fc1"]).T),
        "sfc2T": ac(np.asarray(i["L0_ffn_shared_fc2"]).T),
        "sprojT": ac(np.asarray(i["L0_ffn_shared_proj"]).T),
        "ones_row_f": np.ones((1, 128), f32),
        "constsA": np.broadcast_to(
            np.array([1.0, 1e-5, -1.0, ATT_SCALE, 0, 0, 0, 0], f32), (128, 8)).copy(),
        "ones_col_b": np.ones((128, 1), ml_dtypes.bfloat16),
        "id_f": np.eye(128, dtype=f32),
    }
    g9 = np.concatenate([np.asarray(i["L0_ffn_gate_w"]).T,
                         np.asarray(i["L0_ffn_shared_gate_w"]).T], axis=1)
    sh["gate9T"] = ac(g9.reshape(16, 128, 9).transpose(1, 0, 2).reshape(128, 16 * 9))
    eind = np.zeros((9, 9 * 128), f32)
    for e in range(9):
        eind[e, e * 128:(e + 1) * 128] = 1.0
    sh["eind"] = eind
    return sh


def kernel(**inputs):
    x = np.asarray(inputs["L0_attn_a_x"])
    sh = _prep_shared(inputs)
    nc = build_program(upto=4)
    in_maps = []
    for c in range(8):
        b, s = c // 4, c % 4
        im = dict(sh)
        im["x_in"] = np.ascontiguousarray(x[b, s * TL:(s + 1) * TL, :].T)
        sel = np.zeros((1, 4), np.float32)
        if s > 0:
            sel[0, s - 1] = 1.0
        im["sel"] = sel
        in_maps.append(im)
    res = run_bass_kernel_spmd(nc, in_maps, core_ids=list(range(8)))
    out = np.empty((2, 8192, D), np.float32)
    for c in range(8):
        b, s = c // 4, c % 4
        out[b, s * TL:(s + 1) * TL, :] = res.results[c]["y_out"].T
    return out


# version 09
# speedup vs baseline: 1.1484x; 1.1484x over previous
import numpy as np
import ml_dtypes
import concourse.bass as bass
import concourse.mybir as mybir
from concourse.bass_utils import run_bass_kernel_spmd
from contextlib import ExitStack

F32 = mybir.dt.float32
BF16 = mybir.dt.bfloat16
AF = mybir.ActivationFunctionType
ALU = mybir.AluOpType
AX = mybir.AxisListType

D = 2048
CC = 16
TL = 2048
NH = 16
FM = 1408
FCH = 11
NEXP = 12
TH = 1024
ATT_SCALE = float(1.0 / np.sqrt(128.0))
DEBUG_AA = [False]
GROUPS = [[0, 1, 2, 3], [4, 5, 6, 7]]

_UNIQ = [0]


def _sb(nc, name, shape, dt):
    _UNIQ[0] += 1
    return nc.sbuf_tensor(f"{name}_{_UNIQ[0]}", shape, dt)


def _ps(nc, name, shape, dt):
    _UNIQ[0] += 1
    return nc.psum_tensor(f"{name}_{_UNIQ[0]}", shape, dt)


class MK:
    def __init__(self, nc, es, block, aps):
        self.nc = nc
        self.es = es
        self.blk = block
        self.a = aps
        self.S = {}
        self.c = {}
        for name in ["qs", "qc", "qg", "pe", "dv", "ac", "cc"]:
            self.S[name] = es.enter_context(nc.semaphore("sem_" + name))
            self.c[name] = 0
        self.bar = dict(self.c)
        self.bar_done = set()
        self.sb = {}

    def inc(self, name, instr, n=1):
        instr.then_inc(self.S[name], n)
        self.c[name] += n
        return self.c[name]

    def dma(self, eng, qname, out, in_):
        i = eng.dma_start(out=out, in_=in_)
        return self.inc(qname, i, 16)

    def barrier(self):
        self.bar = dict(self.c)
        self.bar_done = set()

    def w(self, eng, name, v):
        if v > 0:
            eng.wait_ge(self.S[name], v)

    def se(self, eng, instr, name="dv"):
        self.w(eng, name, self.inc(name, instr))

    def sec(self, engname):
        blkfn = getattr(self.blk, engname)

        def deco(f):
            def wrapped(e):
                if engname not in self.bar_done:
                    self.bar_done.add(engname)
                    for k, v in self.bar.items():
                        if v > 0:
                            e.wait_ge(self.S[k], v)
                f(e)
            blkfn(wrapped)
        return deco

    def alloc_persistent(self):
        nc, es, sb = self.nc, self.es, self.sb

        def t(name, shape, dt):
            sb[name] = es.enter_context(nc.sbuf_tensor("sbt_" + name, shape, dt))

        t("ones_row_f", [1, 128], F32)
        t("ones_col_f", [128, 1], F32)
        t("ones_col_b", [128, 1], BF16)
        t("id_f", [128, 128], F32)
        t("gate9", [128, CC * 9], F32)
        t("sel4", [1, 4], F32)
        t("constsA", [128, 8], F32)
        for nm in ["ba", "bx", "lam", "bq", "bk", "bo", "ngam", "bqs"]:
            t(nm, [128, 16], F32)
        t("Rb", [128, TL], F32)
        t("MRb", [128, TL], F32)

    def load_persistent(self):
        mk, sb, a = self, self.sb, self.a

        @mk.sec("sync")
        def _(e):
            for nm, src in [("ones_row_f", "ones_row_f"), ("ones_col_f", "ones_col_f"),
                            ("ones_col_b", "ones_col_b"),
                            ("id_f", "id_f"), ("gate9", "gate9T"),
                            ("sel4", "sel"), ("ba", "ba_c"), ("bx", "bx_c"),
                            ("lam", "lam_c"), ("bq", "bq_c"), ("bk", "bk_c"),
                            ("bo", "bo_c"),
                            ("constsA", "constsA")]:
                mk.dma(e, "qs", sb[nm][:, :], a[src][:, :])
        n_loads = self.c["qs"]

        ph_ps = ExitStack()
        ps_b = ph_ps.enter_context(self.nc.psum_tensor("ps_pers", [128, 512], F32))

        @mk.sec("scalar")
        def _(e):
            mk.w(e, "qs", n_loads)
            mk.se(e, e.activation(out=sb["ngam"][:, :], in_=sb["lam"][:, :], func=AF.Exp,
                                  scale=sb["constsA"][:, 2:3]), "ac")
            mk.se(e, e.activation(out=sb["ngam"][:, :], in_=sb["ngam"][:, :], func=AF.Ln,
                                  bias=sb["constsA"][:, 0:1]), "ac")
            i = e.activation(out=sb["bqs"][:, :], in_=sb["bq"][:, :], func=AF.Identity,
                             scale=sb["constsA"][:, 3:4])
            mk.inc("ac", i)

        @mk.sec("vector")
        def _(e):
            mk.w(e, "ac", self.c["ac"])
            i = e.tensor_scalar(sb["ngam"][:, :], sb["ngam"][:, :], -8.0, None, ALU.mult)
            mk.inc("dv", i)

        self.barrier()
        ph_ps.close()

    def stats_and_rows(self, src_name):
        mk, sb, a, nc = self, self.sb, self.a, self.nc
        src = a[src_name]
        with ExitStack() as ph:
            xb = [ph.enter_context(_sb(nc, f"st_xb{i}", [128, TL], BF16)) for i in range(2)]
            xq = [ph.enter_context(_sb(nc, f"st_xq{i}", [128, TL], BF16)) for i in range(2)]
            rows = {}
            for rn in ["m_row", "v_row", "tmp_row", "rstd_row", "mr_row"]:
                rows[rn] = ph.enter_context(_sb(nc, "st_" + rn, [1, TL], F32))
            with ExitStack() as ps_stk:
                ps_sm = ps_stk.enter_context(_ps(nc, "ps_sm", [1, TL], F32))
                ps_sq = ps_stk.enter_context(_ps(nc, "ps_sq", [1, TL], F32))

                ld = {}; sq = {}; mm = {}
                for cc in range(CC):
                    @mk.sec("gpsimd")
                    def _(e, cc=cc):
                        if cc >= 2:
                            mk.w(e, "pe", mm[cc - 2])
                            mk.w(e, "ac", sq[cc - 2])
                        ld[cc] = mk.dma(e, "qg", xb[cc % 2][:, :],
                                        src[cc * 128:(cc + 1) * 128, :])

                    @mk.sec("scalar")
                    def _(e, cc=cc):
                        mk.w(e, "qg", ld[cc])
                        if cc >= 2:
                            mk.w(e, "pe", mm[cc - 2])
                        i = e.square(xq[cc % 2][:, :], xb[cc % 2][:, :])
                        sq[cc] = mk.inc("ac", i)

                    @mk.sec("tensor")
                    def _(e, cc=cc):
                        mk.w(e, "qg", ld[cc])
                        mk.w(e, "ac", sq[cc])
                        for nt in range(4):
                            e.matmul(ps_sm[0:1, nt * 512:(nt + 1) * 512],
                                     sb["ones_col_b"][:, :],
                                     xb[cc % 2][:, nt * 512:(nt + 1) * 512],
                                     start=(cc == 0), stop=(cc == CC - 1),
                                     skip_group_check=True)
                            i = e.matmul(ps_sq[0:1, nt * 512:(nt + 1) * 512],
                                         sb["ones_col_b"][:, :],
                                         xq[cc % 2][:, nt * 512:(nt + 1) * 512],
                                         start=(cc == 0), stop=(cc == CC - 1),
                                         skip_group_check=True)
                        mm[cc] = mk.inc("pe", i)

                @mk.sec("vector")
                def _(e):
                    mk.w(e, "pe", mm[CC - 1])
                    e.tensor_scalar(rows["m_row"][:, :], ps_sm[0:1, :], 1.0 / D, None, ALU.mult)
                    i = e.tensor_scalar(rows["v_row"][:, :], ps_sq[0:1, :], 1.0 / D, None,
                                        ALU.mult)
                    mk.se(e, i)
                    mk.inc("dv", e.engine_nop())

                @mk.sec("scalar")
                def _(e):
                    mk.w(e, "dv", self.c["dv"])
                    i = e.square(rows["tmp_row"][:, :], rows["m_row"][:, :])
                    mk.inc("ac", i)

                @mk.sec("vector")
                def _(e):
                    mk.w(e, "ac", self.c["ac"])
                    i = e.tensor_tensor(rows["v_row"][:, :], rows["v_row"][:, :],
                                        rows["tmp_row"][:, :], ALU.subtract)
                    mk.inc("dv", i)

                @mk.sec("scalar")
                def _(e):
                    mk.w(e, "dv", self.c["dv"])
                    i = e.activation(out=rows["tmp_row"][:, :], in_=rows["v_row"][:, :],
                                     func=AF.Sqrt, bias=sb["constsA"][0:1, 1:2])
                    mk.inc("ac", i)

                @mk.sec("vector")
                def _(e):
                    mk.w(e, "ac", self.c["ac"])
                    mk.se(e, e.reciprocal(rows["rstd_row"][:, :], rows["tmp_row"][:, :]))
                    i = e.tensor_tensor(rows["mr_row"][:, :], rows["m_row"][:, :],
                                        rows["rstd_row"][:, :], ALU.mult)
                    mk.inc("dv", i)
                self.barrier()

            with ExitStack() as ps_stk:
                ps_bc = ps_stk.enter_context(_ps(nc, "ps_bc", [128, TL], F32))
                for row, dst in [("rstd_row", "Rb"), ("mr_row", "MRb")]:
                    @mk.sec("tensor")
                    def _(e, row=row):
                        mk.w(e, "dv", self.c["dv"])
                        for nt in range(4):
                            i = e.matmul(ps_bc[:, nt * 512:(nt + 1) * 512],
                                         sb["ones_row_f"][:, :],
                                         rows[row][:, nt * 512:(nt + 1) * 512],
                                         start=True, stop=True)
                        mk.inc("pe", i)

                    @mk.sec("vector")
                    def _(e, dst=dst):
                        mk.w(e, "pe", self.c["pe"])
                        i = e.tensor_copy(sb[dst][:, :], ps_bc[:, :])
                        mk.inc("dv", i)
                self.barrier()

    def emit_xn(self, xn, src, width=TL, t0=0, extra=None, xcb=None):
        mk, sb, nc = self, self.sb, self.nc
        with ExitStack() as ph:
            if xcb is None:
                xc = [ph.enter_context(_sb(nc, f"xn_xc{i}", [128, width], F32)) for i in range(2)]
            else:
                xc = xcb
            tt = ph.enter_context(_sb(nc, "xn_tt", [128, width], F32))
            tf = ph.enter_context(_sb(nc, "xn_tf", [128, width], F32))
            dvk = {}
            pe_tf = [0]
            for cc in range(CC):
                @mk.sec("sync")
                def _(e, cc=cc):
                    if cc >= 2:
                        mk.w(e, "dv", dvk[cc - 2])
                    mk.dma(e, "qs", xc[cc % 2][:, :],
                           src[cc * 128:(cc + 1) * 128, t0:t0 + width])
                qsl = self.c["qs"]

                @mk.sec("vector")
                def _(e, cc=cc, qsl=qsl):
                    mk.w(e, "qs", qsl)
                    mk.se(e, e.tensor_tensor(tt[:, :], xc[cc % 2][:, :],
                                             sb["Rb"][:, t0:t0 + width], ALU.mult))
                    if extra is None:
                        i = e.tensor_tensor(xn[:, cc * width:(cc + 1) * width], tt[:, :],
                                            sb["MRb"][:, t0:t0 + width], ALU.subtract)
                    else:
                        mk.w(e, "pe", pe_tf[0])
                        mk.se(e, e.tensor_tensor(tf[:, :], tt[:, :],
                                                 sb["MRb"][:, t0:t0 + width], ALU.subtract))
                        i = e.tensor_copy(xn[:, cc * width:(cc + 1) * width], tf[:, :])
                    dvk[cc] = mk.inc("dv", i)
                if extra is not None:
                    pe_tf[0] = extra(cc, tf, dvk[cc])
            self.barrier()

    def attn_a(self):
        mk, sb, a, nc = self, self.sb, self.a, self.nc
        with ExitStack() as ph:
            xn = ph.enter_context(_sb(nc, "aa_xn", [128, CC * TL], BF16))
            self.emit_xn(xn, a["x_in"])

            wbuf = [ph.enter_context(_sb(nc, f"aa_w{i}", [128, 2 * CC * 128], BF16))
                    for i in range(2)]
            xc = [ph.enter_context(_sb(nc, f"aa_xc{i}", [128, TL], F32)) for i in range(2)]
            tmp1 = ph.enter_context(_sb(nc, "aa_t1", [128, TL], F32))
            tmp2 = ph.enter_context(_sb(nc, "aa_t2", [128, TL], F32))
            r_sb = ph.enter_context(_sb(nc, "aa_r", [128, TL], F32))
            i_sb = ph.enter_context(_sb(nc, "aa_i", [128, TL], F32))
            a_sb = ph.enter_context(_sb(nc, "aa_a", [128, TL], F32))
            sq_sb = ph.enter_context(_sb(nc, "aa_sq", [128, TL], F32))
            u_sb = ph.enter_context(_sb(nc, "aa_u", [128, TL], F32))
            y_sb = ph.enter_context(_sb(nc, "aa_y", [128, TL], F32))
            st32 = ph.enter_context(_sb(nc, "aa_st", [128, 32], F32))
            gt = [ph.enter_context(_sb(nc, f"aa_g{i}", [128, 32], F32)) for i in range(4)]
            P1 = ph.enter_context(_sb(nc, "aa_P1", [128, 16], F32))
            P2 = ph.enter_context(_sb(nc, "aa_P2", [128, 16], F32))
            hin = ph.enter_context(_sb(nc, "aa_hin", [128, 16], F32))
            selb = ph.enter_context(_sb(nc, "aa_selb", [128, 4], F32))
            ps_r = ph.enter_context(_ps(nc, "aa_psr", [128, TL], F32))
            ps_i = ph.enter_context(_ps(nc, "aa_psi", [128, TL], F32))

            wld = {}; pemm = {}; acsig = {}; aca = {}; dvloga = {}; dvu = {}
            dvscan = {}; qsx = {}; spq = {}

            for step in range(CC + 1):
                if step < CC:
                    @mk.sec("gpsimd")
                    def _(e, hh=step):
                        if hh >= 2:
                            mk.w(e, "pe", pemm[hh - 2])
                        for mi, wn in [(0, "WaT"), (1, "WxT")]:
                            src = a[wn][:, hh * 128:(hh + 1) * 128]
                            src = src.rearrange("(cc p) h -> p cc h", p=128)
                            dst = wbuf[hh % 2][:, mi * CC * 128:(mi + 1) * CC * 128]
                            dst = dst.rearrange("p (cc h) -> p cc h", h=128)
                            mk.dma(e, "qg", dst, src)
                        wld[hh] = self.c["qg"]
                if step == 0:
                    continue
                h = step - 1

                @mk.sec("tensor")
                def _(e, h=h):
                    mk.w(e, "qg", wld[h])
                    if h >= 1:
                        mk.w(e, "ac", acsig[h - 1])
                    for mi, ps in [(0, ps_r), (1, ps_i)]:
                        for cc in range(CC):
                            for nt in range(4):
                                i = e.matmul(
                                    ps[:, nt * 512:(nt + 1) * 512],
                                    wbuf[h % 2][:, (mi * CC + cc) * 128:(mi * CC + cc + 1) * 128],
                                    xn[:, cc * TL + nt * 512: cc * TL + (nt + 1) * 512],
                                    start=(cc == 0), stop=(cc == CC - 1),
                                    skip_group_check=True)
                    pemm[h] = mk.inc("pe", i)

                @mk.sec("scalar")
                def _(e, h=h):
                    mk.w(e, "pe", pemm[h])
                    if h >= 1:
                        mk.w(e, "dv", dvu[h - 1])
                    e.activation(out=r_sb[:, :], in_=ps_r[:, :], func=AF.Sigmoid,
                                 bias=sb["ba"][:, h:h + 1])
                    i = e.activation(out=i_sb[:, :], in_=ps_i[:, :], func=AF.Sigmoid,
                                     bias=sb["bx"][:, h:h + 1])
                    acsig[h] = mk.inc("ac", i)

                @mk.sec("vector")
                def _(e, h=h):
                    mk.w(e, "ac", acsig[h])
                    i = e.tensor_scalar(tmp1[:, :], r_sb[:, :], sb["ngam"][:, h:h + 1],
                                        None, ALU.mult)
                    dvloga[h] = mk.inc("dv", i)

                @mk.sec("scalar")
                def _(e, h=h):
                    mk.w(e, "dv", dvloga[h])
                    if h >= 1:
                        mk.w(e, "qs", spq[h - 1])
                    mk.se(e, e.activation(out=a_sb[:, :], in_=tmp1[:, :], func=AF.Exp), "ac")
                    mk.se(e, e.square(tmp1[:, :], a_sb[:, :]), "ac")
                    i = e.activation(out=sq_sb[:, :], in_=tmp1[:, :], func=AF.Sqrt,
                                     scale=sb["constsA"][:, 2:3], bias=sb["constsA"][:, 0:1])
                    aca[h] = mk.inc("ac", i)

                @mk.sec("sync")
                def _(e, h=h):
                    if h >= 2:
                        mk.w(e, "dv", dvu[h - 2])
                    qsx[h] = mk.dma(e, "qs", xc[h % 2][:, :],
                                    a["x_in"][h * 128:(h + 1) * 128, :])

                @mk.sec("vector")
                def _(e, h=h):
                    mk.w(e, "qs", qsx[h])
                    mk.w(e, "ac", aca[h])
                    if h >= 1:
                        mk.w(e, "qs", spq[h - 1])
                    mk.se(e, e.tensor_tensor(tmp2[:, :], xc[h % 2][:, :], sb["Rb"][:, :],
                                             ALU.mult))
                    mk.se(e, e.tensor_tensor(tmp2[:, :], tmp2[:, :], sb["MRb"][:, :],
                                             ALU.subtract))
                    mk.se(e, e.tensor_tensor(tmp2[:, :], i_sb[:, :], tmp2[:, :], ALU.mult))
                    i = e.tensor_tensor(u_sb[:, :], tmp2[:, :], sq_sb[:, :], ALU.mult)
                    dvu[h] = mk.inc("dv", i)
                    mk.w(e, "dv", dvu[h])
                    e.tensor_reduce(st32[:, h:h + 1], a_sb[:, :], AX.X, ALU.mult)
                    i = e.tensor_tensor_scan(y_sb[:, :], a_sb[:, :], u_sb[:, :], 0.0,
                                             ALU.mult, ALU.add)
                    mk.w(e, "dv", mk.inc("dv", i))
                    i = e.tensor_copy(st32[:, 16 + h:17 + h], y_sb[:, TL - 1:TL])
                    dvscan[h] = mk.inc("dv", i)

                @mk.sec("sync")
                def _(e, h=h):
                    mk.w(e, "dv", dvscan[h])
                    mk.dma(e, "qs", a["sp_a"][h * 128:(h + 1) * 128, :], a_sb[:, :])
                    spq[h] = mk.dma(e, "qs", a["sp_u"][h * 128:(h + 1) * 128, :], u_sb[:, :])
                    if DEBUG_AA[0] is not False and h == DEBUG_AA[0]:
                        mk.dma(e, "qs", a["dbg_a"][:, :], a_sb[:, :])
                        mk.dma(e, "qs", a["dbg_u"][:, :], u_sb[:, :])
                        mk.dma(e, "qs", a["dbg_r"][:, :], i_sb[:, :])
                        mk.dma(e, "qs", a["dbg_sq"][:, :], sq_sb[:, :])
                        spq[h] = mk.dma(e, "qs", a["dbg_y"][:, :], y_sb[:, :])

            @mk.sec("sync")
            def _(e):
                mk.w(e, "dv", dvscan[CC - 1])
                mk.dma(e, "qs", a["cc_scan_in"][:, :], st32[:, :])
            qs_sc = self.c["qs"]

            @mk.sec("gpsimd")
            def _(e):
                mk.w(e, "qs", qs_sc)
                i = e.collective_compute(
                    "AllGather", ALU.bypass, replica_groups=GROUPS,
                    ins=[a["cc_scan_in"].opt()], outs=[a["cc_scan_out"].opt()])
                mk.inc("cc", i)
            cc_sc = self.c["cc"]

            @mk.sec("sync")
            def _(e):
                mk.w(e, "cc", cc_sc)
                for r in range(4):
                    mk.dma(e, "qs", gt[r][:, :], a["cc_scan_out"][r * 128:(r + 1) * 128, :])
            qs_gt = self.c["qs"]

            @mk.sec("tensor")
            def _(e):
                mk.w(e, "ac", acsig[CC - 1])
                i = e.matmul(ps_r[:, 0:4], sb["ones_row_f"][:, :], sb["sel4"][:, :],
                             start=True, stop=True)
                mk.inc("pe", i)
            pe_sel = self.c["pe"]

            @mk.sec("vector")
            def _(e):
                mk.w(e, "pe", pe_sel)
                mk.w(e, "qs", qs_gt)
                mk.se(e, e.tensor_copy(selb[:, :], ps_r[:, 0:4]))
                mk.se(e, e.tensor_tensor(P1[:, :], gt[1][:, 0:16], gt[0][:, 16:32], ALU.mult))
                mk.se(e, e.tensor_tensor(P1[:, :], P1[:, :], gt[1][:, 16:32], ALU.add))
                mk.se(e, e.tensor_tensor(P2[:, :], gt[2][:, 0:16], P1[:, :], ALU.mult))
                mk.se(e, e.tensor_tensor(P2[:, :], P2[:, :], gt[2][:, 16:32], ALU.add))
                mk.se(e, e.tensor_scalar(hin[:, :], gt[0][:, 16:32], selb[:, 0:1], None,
                                         ALU.mult))
                mk.se(e, e.tensor_scalar(gt[3][:, 0:16], P1[:, :], selb[:, 1:2], None,
                                         ALU.mult))
                mk.se(e, e.tensor_tensor(hin[:, :], hin[:, :], gt[3][:, 0:16], ALU.add))
                mk.se(e, e.tensor_scalar(gt[3][:, 0:16], P2[:, :], selb[:, 2:3], None,
                                         ALU.mult))
                i = e.tensor_tensor(hin[:, :], hin[:, :], gt[3][:, 0:16], ALU.add)
                mk.inc("dv", i)
            dv_hin = self.c["dv"]

            if DEBUG_AA[0]:
                @mk.sec("sync")
                def _(e):
                    mk.w(e, "dv", dv_hin)
                    mk.dma(e, "qs", a["dbg_st"][:, :], st32[:, :])
                    mk.dma(e, "qs", a["dbg_hin"][:, :], hin[:, :])
                    mk.dma(e, "qs", a["dbg_selb"][:, :], selb[:, :])

            ldq = {}; dvs2 = {}; stq = {}
            bufA = [(a_sb, u_sb), (r_sb, i_sb)]
            for h in range(CC):
                @mk.sec("sync")
                def _(e, h=h):
                    if h == 0:
                        mk.w(e, "dv", dv_hin)
                        mk.w(e, "qs", spq[CC - 1])
                    if h >= 2:
                        mk.w(e, "dv", dvs2[h - 2])
                    A, U = bufA[h % 2]
                    mk.dma(e, "qs", A[:, :], a["sp_a"][h * 128:(h + 1) * 128, :])
                    mk.dma(e, "qs", U[:, :], a["sp_u"][h * 128:(h + 1) * 128, :])
                    ldq[h] = mk.dma(e, "qs", xc[h % 2][:, :],
                                    a["x_in"][h * 128:(h + 1) * 128, :])

                @mk.sec("vector")
                def _(e, h=h):
                    mk.w(e, "qs", ldq[h])
                    if h >= 1:
                        mk.w(e, "qs", stq[h - 1])
                    A, U = bufA[h % 2]
                    mk.se(e, e.tensor_tensor_scan(y_sb[:, :], A[:, :], U[:, :],
                                                  hin[:, h:h + 1], ALU.mult, ALU.add))
                    i = e.tensor_tensor(y_sb[:, :], y_sb[:, :], xc[h % 2][:, :], ALU.add)
                    dvs2[h] = mk.inc("dv", i)

                @mk.sec("sync")
                def _(e, h=h):
                    mk.w(e, "dv", dvs2[h])
                    stq[h] = mk.dma(e, "qs", a["x_fm"][h * 128:(h + 1) * 128, :], y_sb[:, :])
            self.barrier()

    def ffn(self, dst_name):
        mk, sb, a, nc = self, self.sb, self.a, self.nc
        with ExitStack() as ph:
            xf = ph.enter_context(_sb(nc, "ff_xf", [128, CC * TH], BF16))
            y_sb = ph.enter_context(_sb(nc, "ff_y", [128, CC * TH], F32))
            h_e = ph.enter_context(_sb(nc, "ff_h", [128, FCH * TH], BF16))
            B_sb = ph.enter_context(_sb(nc, "ff_B", [128, TH], F32))
            cmb = ph.enter_context(_sb(nc, "ff_cmb", [9, TH], F32))
            rt_sb = ph.enter_context(_sb(nc, "ff_rt", [9, TH], F32))
            lgt = ph.enter_context(_sb(nc, "ff_lgt", [128, 9], F32))
            msk = ph.enter_context(_sb(nc, "ff_msk", [128, 16], F32))
            srow = ph.enter_context(_sb(nc, "ff_srow", [128, 2], F32))
            w1 = [ph.enter_context(_sb(nc, f"ff_w1{i}", [128, CC * 128], BF16))
                  for i in range(2)]
            w2 = [ph.enter_context(_sb(nc, f"ff_w2{i}", [128, CC * 128], BF16))
                  for i in range(2)]
            wp = [ph.enter_context(_sb(nc, f"ff_wp{i}", [128, FCH * 256], BF16))
                  for i in range(2)]
            sil = ph.enter_context(_sb(nc, "ff_sil", [128, TH], BF16))
            h2s = ph.enter_context(_sb(nc, "ff_h2s", [128, TH], BF16))
            xc = [ph.enter_context(_sb(nc, f"ff_xc{i}", [128, TH], F32)) for i in range(2)]
            eind = ph.enter_context(_sb(nc, "ff_eind", [9, 9 * 128], F32))
            P = [ph.enter_context(_ps(nc, f"ff_P{i}", [128, TH], F32)) for i in range(4)]

            @mk.sec("sync")
            def _(e):
                mk.dma(e, "qs", eind[:, :], a["eind"][:, :])

            st_half = {-1: 0, 0: 0}
            for hp in range(2):
                t0 = hp * TH
                rtk = [0]

                def router_cb(cc, tf, dvt, rtk=rtk):
                    @mk.sec("tensor")
                    def _(e, cc=cc):
                        mk.w(e, "dv", dvt)
                        if cc == 0:
                            mk.w(e, "dv", self.c["dv"])
                        for nt in range(2):
                            i = e.matmul(P[0][0:9, nt * 512:(nt + 1) * 512],
                                         sb["gate9"][:, cc * 9:(cc + 1) * 9],
                                         tf[:, nt * 512:(nt + 1) * 512],
                                         start=(cc == 0), stop=(cc == CC - 1),
                                         skip_group_check=True)
                        rtk[0] = mk.inc("pe", i)
                    return rtk[0]

                self.emit_xn(xf, a["x_fm"], width=TH, t0=t0, extra=router_cb, xcb=xc)

                @mk.sec("vector")
                def _(e):
                    mk.w(e, "pe", rtk[0])
                    i = e.tensor_copy(rt_sb[:, :], P[0][0:9, :])
                    mk.inc("dv", i)

                for tt in range(8):
                    @mk.sec("tensor")
                    def _(e, tt=tt):
                        mk.w(e, "dv", self.c["dv"])
                        i = e.transpose(P[1][:, 0:9], rt_sb[0:9, tt * 128:(tt + 1) * 128],
                                        sb["id_f"][0:9, 0:9])
                        mk.inc("pe", i)

                    @mk.sec("vector")
                    def _(e, tt=tt):
                        mk.w(e, "pe", self.c["pe"])
                        mk.se(e, e.tensor_copy(lgt[:, :], P[1][:, 0:9]))
                        mk.se(e, e.reduce_max(srow[:, 0:1], lgt[:, 0:8], AX.X))
                        i = e.tensor_scalar(srow[:, 1:2], srow[:, 0:1], -1.0, None, ALU.mult)
                        mk.inc("dv", i)

                    @mk.sec("scalar")
                    def _(e, tt=tt):
                        mk.w(e, "dv", self.c["dv"])
                        e.activation(out=lgt[:, 0:8], in_=lgt[:, 0:8], func=AF.Exp,
                                     bias=srow[:, 1:2], accum_out=srow[:, 0:1])
                        i = e.activation(out=lgt[:, 8:9], in_=lgt[:, 8:9], func=AF.Sigmoid)
                        mk.inc("ac", i)

                    @mk.sec("vector")
                    def _(e, tt=tt):
                        mk.w(e, "ac", self.c["ac"])
                        mk.se(e, e.reciprocal(srow[:, 1:2], srow[:, 0:1]))
                        mk.se(e, e.tensor_scalar(lgt[:, 0:8], lgt[:, 0:8], srow[:, 1:2], None,
                                                 ALU.mult))
                        mk.se(e, e.reduce_max(srow[:, 0:1], lgt[:, 0:8], AX.X))
                        mk.se(e, e.tensor_scalar(msk[:, 0:8], lgt[:, 0:8], srow[:, 0:1], None,
                                                 ALU.is_ge))
                        mk.se(e, e.scalar_tensor_tensor(msk[:, 8:16], msk[:, 0:8], -1e30,
                                                        lgt[:, 0:8], ALU.mult, ALU.add))
                        mk.se(e, e.reduce_max(srow[:, 1:2], msk[:, 8:16], AX.X))
                        mk.se(e, e.tensor_scalar(msk[:, 0:8], lgt[:, 0:8], srow[:, 1:2], None,
                                                 ALU.is_ge))
                        i = e.tensor_tensor(lgt[:, 0:8], lgt[:, 0:8], msk[:, 0:8], ALU.mult)
                        mk.inc("dv", i)

                    @mk.sec("tensor")
                    def _(e, tt=tt):
                        mk.w(e, "dv", self.c["dv"])
                        i = e.transpose(P[2][0:9, 0:128], lgt[:, 0:9], sb["id_f"][:, :])
                        mk.inc("pe", i)

                    @mk.sec("vector")
                    def _(e, tt=tt):
                        mk.w(e, "pe", self.c["pe"])
                        i = e.tensor_copy(cmb[:, tt * 128:(tt + 1) * 128], P[2][0:9, 0:128])
                        mk.inc("dv", i)

                pefc = {}; acsil = {}; dvh = {}
                pepj = {}; dvy = {}
                dvB = 0
                for ex in range(NEXP):
                    @mk.sec("tensor")
                    def _(e, ex=ex):
                        mk.w(e, "dv", self.c["dv"])
                        ei = ex if ex < 8 else 8
                        for nt in range(2):
                            i = e.matmul(P[3][:, nt * 512:(nt + 1) * 512],
                                         eind[:, ei * 128:(ei + 1) * 128],
                                         cmb[0:9, nt * 512:(nt + 1) * 512],
                                         start=True, stop=True, skip_group_check=True)
                        mk.inc("pe", i)

                    @mk.sec("vector")
                    def _(e, ex=ex):
                        mk.w(e, "pe", self.c["pe"])
                        i = e.tensor_copy(B_sb[:, :], P[3][:, :])
                        mk.se(e, i)
                        mk.inc("dv", e.engine_nop())
                    dvB = self.c["dv"]

                    for fc in range(FCH):
                        k = ex * FCH + fc

                        @mk.sec("gpsimd")
                        def _(e, ex=ex, fc=fc, k=k):
                            if k >= 2:
                                mk.w(e, "pe", pefc[k - 2])
                            for wn, dstb in [("fc1", w1[fc % 2]), ("fc2", w2[fc % 2])]:
                                if ex < 8:
                                    src = a[wn + "T"][ex, :, fc * 128:(fc + 1) * 128]
                                else:
                                    c0 = (ex - 8) * FM + fc * 128
                                    src = a["s" + wn + "T"][:, c0:c0 + 128]
                                src = src.rearrange("(cc p) f -> p cc f", p=128)
                                dst = dstb[:, :].rearrange("p (cc f) -> p cc f", f=128)
                                mk.dma(e, "qg", dst, src)
                        wldk = self.c["qg"]

                        @mk.sec("tensor")
                        def _(e, ex=ex, fc=fc, k=k, wldk=wldk, dvB=dvB):
                            mk.w(e, "qg", wldk)
                            if k >= 2:
                                mk.w(e, "ac", acsil[k - 2])
                                mk.w(e, "dv", dvh[k - 2])
                            else:
                                mk.w(e, "dv", dvB)
                            pA = P[0] if fc % 2 == 0 else P[2]
                            pB = P[1] if fc % 2 == 0 else P[3]
                            for wb, ps in [(w1[fc % 2], pA), (w2[fc % 2], pB)]:
                                for cc in range(CC):
                                    for nt in range(2):
                                        i = e.matmul(
                                            ps[:, nt * 512:(nt + 1) * 512],
                                            wb[:, cc * 128:(cc + 1) * 128],
                                            xf[:, cc * TH + nt * 512: cc * TH + (nt + 1) * 512],
                                            start=(cc == 0), stop=(cc == CC - 1),
                                            skip_group_check=True)
                            pefc[k] = mk.inc("pe", i)

                        @mk.sec("scalar")
                        def _(e, fc=fc, k=k):
                            mk.w(e, "pe", pefc[k])
                            if k >= 1:
                                mk.w(e, "dv", dvh[k - 1])
                            pA = P[0] if fc % 2 == 0 else P[2]
                            i = e.activation(out=sil[:, :], in_=pA[:, :], func=AF.Silu)
                            acsil[k] = mk.inc("ac", i)

                        @mk.sec("vector")
                        def _(e, fc=fc, k=k, dvB=dvB):
                            mk.w(e, "pe", pefc[k])
                            pB = P[1] if fc % 2 == 0 else P[3]
                            mk.se(e, e.tensor_tensor(h2s[:, :], pB[:, :], B_sb[:, :], ALU.mult))
                            mk.w(e, "ac", acsil[k])
                            i = e.tensor_tensor(h_e[:, fc * TH:(fc + 1) * TH], sil[:, :],
                                                h2s[:, :], ALU.mult)
                            dvh[k] = mk.inc("dv", i)

                    for q in range(8):
                        kq = ex * 8 + q

                        @mk.sec("gpsimd")
                        def _(e, ex=ex, q=q, kq=kq):
                            if kq >= 2:
                                mk.w(e, "pe", pepj[kq - 2])
                            if ex < 8:
                                src = a["projT"][ex, :, q * 256:(q + 1) * 256]
                            else:
                                src = a["sprojT"][(ex - 8) * FM:(ex - 7) * FM,
                                                  q * 256:(q + 1) * 256]
                            src = src.rearrange("(fc p) c -> p fc c", p=128)
                            dst = wp[q % 2][:, :].rearrange("p (fc c) -> p fc c", c=256)
                            mk.dma(e, "qg", dst, src)
                        wldq = self.c["qg"]

                        for cc4 in range(2):
                            ccg = q * 2 + cc4

                            @mk.sec("tensor")
                            def _(e, ex=ex, q=q, kq=kq, cc4=cc4, wldq=wldq):
                                mk.w(e, "qg", wldq)
                                mk.w(e, "dv", dvh[ex * FCH + FCH - 1])
                                ps = P[0] if cc4 % 2 == 0 else P[1]
                                war = dvy.get((ex, q - 1, cc4))
                                if war is None and ex > 0:
                                    war = dvy.get((ex - 1, 7, cc4))
                                if war:
                                    mk.w(e, "dv", war)
                                for fc in range(FCH):
                                    for nt in range(2):
                                        i = e.matmul(
                                            ps[:, nt * 512:(nt + 1) * 512],
                                            wp[q % 2][:, fc * 256 + cc4 * 128:
                                                      fc * 256 + (cc4 + 1) * 128],
                                            h_e[:, fc * TH + nt * 512: fc * TH + (nt + 1) * 512],
                                            start=(fc == 0), stop=(fc == FCH - 1),
                                            skip_group_check=True)
                                pepj[kq] = mk.inc("pe", i)

                            @mk.sec("vector")
                            def _(e, ex=ex, q=q, cc4=cc4, ccg=ccg, hp=hp):
                                mk.w(e, "pe", self.c["pe"])
                                ps = P[0] if cc4 % 2 == 0 else P[1]
                                if ex == 0:
                                    mk.w(e, "qs", st_half[hp - 1])
                                    i = e.tensor_copy(y_sb[:, ccg * TH:(ccg + 1) * TH], ps[:, :])
                                else:
                                    i = e.tensor_tensor(y_sb[:, ccg * TH:(ccg + 1) * TH],
                                                        y_sb[:, ccg * TH:(ccg + 1) * TH],
                                                        ps[:, :], ALU.add)
                                dvy[(ex, q, cc4)] = mk.inc("dv", i)

                dvres = {}
                for cc in range(CC):
                    @mk.sec("sync")
                    def _(e, cc=cc):
                        if cc >= 2:
                            mk.w(e, "dv", dvres[cc - 2])
                        mk.dma(e, "qs", xc[cc % 2][:, :],
                               a["x_fm"][cc * 128:(cc + 1) * 128, t0:t0 + TH])
                    qres = self.c["qs"]

                    @mk.sec("vector")
                    def _(e, cc=cc, qres=qres):
                        mk.w(e, "qs", qres)
                        i = e.tensor_tensor(y_sb[:, cc * TH:(cc + 1) * TH],
                                            y_sb[:, cc * TH:(cc + 1) * TH],
                                            xc[cc % 2][:, :], ALU.add)
                        dvres[cc] = mk.inc("dv", i)

                    @mk.sec("sync")
                    def _(e, cc=cc):
                        mk.w(e, "dv", dvres[cc])
                        mk.dma(e, "qs", a[dst_name][cc * 128:(cc + 1) * 128, t0:t0 + TH],
                               y_sb[:, cc * TH:(cc + 1) * TH])
                st_half[hp] = self.c["qs"]
            self.barrier()

    def attn_b(self):
        mk, sb, a, nc = self, self.sb, self.a, self.nc
        with ExitStack() as ph:
            q_bf = ph.enter_context(_sb(nc, "ab_q", [128, CC * TL], BF16))
            negB = ph.enter_context(_sb(nc, "ab_negB", [128, 16], F32))
            qn = ph.enter_context(_sb(nc, "ab_qn", [1, 16], F32))
            kn = ph.enter_context(_sb(nc, "ab_kn", [1, 16], F32))
            kng = [ph.enter_context(_sb(nc, f"ab_kng{i}", [1, 16], F32)) for i in range(4)]

            with ExitStack() as s1:
                bvb = s1.enter_context(_sb(nc, "ab_bvb", [128, D], F32))
                with ExitStack() as s0:
                    ps_b = s0.enter_context(_ps(nc, "ab_psb", [128, 512], F32))
                    bvr = s0.enter_context(_sb(nc, "ab_bvr", [1, D], F32))

                    @mk.sec("sync")
                    def _(e):
                        mk.dma(e, "qs", bvr[:, :], a["bv_row"][:, :])
                    nl = self.c["qs"]
                    for nt in range(4):
                        @mk.sec("tensor")
                        def _(e, nt=nt, nl=nl):
                            if nt == 0:
                                mk.w(e, "qs", nl)
                            else:
                                mk.w(e, "dv", self.c["dv"])
                            i = e.matmul(ps_b[:, :], sb["ones_row_f"][:, :],
                                         bvr[:, nt * 512:(nt + 1) * 512], start=True, stop=True)
                            mk.inc("pe", i)

                        @mk.sec("vector")
                        def _(e, nt=nt):
                            mk.w(e, "pe", self.c["pe"])
                            i = e.tensor_copy(bvb[:, nt * 512:(nt + 1) * 512], ps_b[:, :])
                            mk.inc("dv", i)
                    self.barrier()


                xn = s1.enter_context(_sb(nc, "ab_xn", [128, CC * TL], BF16))
                self.emit_xn(xn, a["x_fm"])

                wbuf = [s1.enter_context(_sb(nc, f"ab_w{i}", [128, CC * 128], BF16))
                        for i in range(2)]
                kev = [s1.enter_context(_sb(nc, f"ab_kev{i}", [128, TL], BF16))
                       for i in range(2)]
                ksq = s1.enter_context(_sb(nc, "ab_ksq", [128, TL], BF16))
                wv = s1.enter_context(_sb(nc, "ab_wv", [128, CC * 512], BF16))

                def proj_loop(wname, brow, scale, store, qdst, nrow):
                    wld = {}; pemm = {}; acev = {}; dvnm = {}; stq = {}
                    with ExitStack() as s2:
                        ps_k = s2.enter_context(_ps(nc, "ab_psk", [128, TL], F32))
                        ps_n = s2.enter_context(_ps(nc, "ab_psn", [1, TL], F32))
                        for step in range(CC + 1):
                            if step < CC:
                                @mk.sec("gpsimd")
                                def _(e, hh=step):
                                    if hh >= 2:
                                        mk.w(e, "pe", pemm[hh - 2])
                                    src = a[wname][:, hh * 128:(hh + 1) * 128]
                                    src = src.rearrange("(cc p) h -> p cc h", p=128)
                                    dst = wbuf[hh % 2][:, :].rearrange("p (cc h) -> p cc h",
                                                                       h=128)
                                    mk.dma(e, "qg", dst, src)
                                    wld[hh] = self.c["qg"]
                            if step == 0:
                                continue
                            h = step - 1

                            @mk.sec("tensor")
                            def _(e, h=h):
                                mk.w(e, "qg", wld[h])
                                if h >= 1:
                                    mk.w(e, "ac", acev[h - 1])
                                for cc in range(CC):
                                    for nt in range(4):
                                        i = e.matmul(
                                            ps_k[:, nt * 512:(nt + 1) * 512],
                                            wbuf[h % 2][:, cc * 128:(cc + 1) * 128],
                                            xn[:, cc * TL + nt * 512: cc * TL + (nt + 1) * 512],
                                            start=(cc == 0), stop=(cc == CC - 1),
                                            skip_group_check=True)
                                pemm[h] = mk.inc("pe", i)

                            @mk.sec("scalar")
                            def _(e, h=h):
                                mk.w(e, "pe", pemm[h])
                                if h >= 2 and store:
                                    mk.w(e, "qs", stq[h - 2])
                                if h >= 1:
                                    mk.w(e, "dv", dvnm[h - 1])
                                dst = (kev[h % 2][:, :] if store
                                       else q_bf[:, h * TL:(h + 1) * TL])
                                if scale is None:
                                    i = e.activation(out=dst, in_=ps_k[:, :], func=AF.Identity,
                                                     bias=brow[:, h:h + 1])
                                else:
                                    i = e.activation(out=dst, in_=ps_k[:, :], func=AF.Identity,
                                                     scale=scale, bias=brow[:, h:h + 1])
                                mk.se(e, i, "ac")
                                i = e.square(ksq[:, :], dst)
                                acev[h] = mk.inc("ac", i)

                            @mk.sec("tensor")
                            def _(e, h=h):
                                mk.w(e, "ac", acev[h])
                                if h >= 1:
                                    mk.w(e, "dv", dvnm[h - 1])
                                for nt in range(4):
                                    i = e.matmul(ps_n[0:1, nt * 512:(nt + 1) * 512],
                                                 sb["ones_col_b"][:, :],
                                                 ksq[:, nt * 512:(nt + 1) * 512],
                                                 start=True, stop=True, skip_group_check=True)
                                mk.inc("pe", i)

                            @mk.sec("vector")
                            def _(e, h=h):
                                mk.w(e, "pe", self.c["pe"])
                                i = e.reduce_max(nrow[0:1, h:h + 1], ps_n[0:1, :], AX.X)
                                dvnm[h] = mk.inc("dv", i)

                            if store:
                                @mk.sec("sync")
                                def _(e, h=h):
                                    mk.w(e, "ac", acev[h])
                                    stq[h] = mk.dma(e, "qs",
                                                    a["cc_kv_in"][h * 128:(h + 1) * 128, :],
                                                    kev[h % 2][:, :])
                        self.barrier()

                proj_loop("wkT", sb["bk"], None, True, None, kn)

                @mk.sec("sync")
                def _(e):
                    mk.dma(e, "qs", a["cc_kn_in"][:, :], kn[:, :])
                qs_kn = self.c["qs"]

                @mk.sec("gpsimd")
                def _(e):
                    mk.w(e, "qs", qs_kn)
                    i = e.collective_compute(
                        "AllGather", ALU.bypass, replica_groups=GROUPS,
                        ins=[a["cc_kn_in"].opt()], outs=[a["cc_kn_out"].opt()])
                    mk.inc("cc", i)
                cc_kn = self.c["cc"]

                with ExitStack() as s2:
                    ps_v = [s2.enter_context(_ps(nc, f"ab_psv{i}", [128, 512], F32))
                            for i in range(2)]
                    wvld = {}; pev = {}; dvv = {}; stv = {}
                    for chb in range(4):
                        @mk.sec("gpsimd")
                        def _(e, chb=chb):
                            if chb >= 1:
                                mk.w(e, "pe", pev[(chb - 1, CC - 1)])
                            src = a["wvT"][:, chb * 512:(chb + 1) * 512]
                            src = src.rearrange("(cc p) h -> p cc h", p=128)
                            dst = wv[:, :].rearrange("p (cc h) -> p cc h", h=512)
                            mk.dma(e, "qg", dst, src)
                        wvld[chb] = self.c["qg"]

                        for tt in range(CC):
                            @mk.sec("tensor")
                            def _(e, chb=chb, tt=tt):
                                mk.w(e, "qg", wvld[chb])
                                war = dvv.get((chb, tt - 2)) or (dvv.get((chb - 1, CC - 2 + tt))
                                                                 if chb >= 1 else 0)
                                if war:
                                    mk.w(e, "dv", war)
                                for cc in range(CC):
                                    i = e.matmul(
                                        ps_v[tt % 2][:, :],
                                        xn[:, cc * TL + tt * 128: cc * TL + (tt + 1) * 128],
                                        wv[:, cc * 512:(cc + 1) * 512],
                                        start=(cc == 0), stop=(cc == CC - 1),
                                        skip_group_check=True)
                                pev[(chb, tt)] = mk.inc("pe", i)

                            @mk.sec("vector")
                            def _(e, chb=chb, tt=tt):
                                mk.w(e, "pe", pev[(chb, tt)])
                                if stv.get((chb, tt - 2)) or (chb >= 1):
                                    mk.w(e, "qs", stv.get((chb, tt - 2),
                                                          stv.get((chb - 1, CC - 2 + tt), 0)))
                                i = e.tensor_tensor(kev[tt % 2][:, 0:512], ps_v[tt % 2][:, :],
                                                    bvb[:, chb * 512:(chb + 1) * 512],
                                                    ALU.add)
                                dvv[(chb, tt)] = mk.inc("dv", i)

                            @mk.sec("sync")
                            def _(e, chb=chb, tt=tt):
                                mk.w(e, "dv", dvv[(chb, tt)])
                                stv[(chb, tt)] = mk.dma(
                                    e, "qs",
                                    a["cc_kv_in"][TL + tt * 128:TL + (tt + 1) * 128,
                                                  chb * 512:(chb + 1) * 512],
                                    kev[tt % 2][:, 0:512])
                    self.barrier()

                @mk.sec("gpsimd")
                def _(e):
                    i = e.collective_compute(
                        "AllGather", ALU.bypass, replica_groups=GROUPS,
                        ins=[a["cc_kv_in"].opt()], outs=[a["cc_kv_out"].opt()])
                    mk.inc("cc", i)
                cc_kv = self.c["cc"]

                proj_loop("wqT", sb["bqs"], sb["constsA"][:, 3:4], False, q_bf, qn)

                @mk.sec("sync")
                def _(e):
                    mk.w(e, "cc", cc_kn)
                    for r in range(4):
                        mk.dma(e, "qs", kng[r][:, :], a["cc_kn_out"][r:r + 1, :])
                qs_kg = self.c["qs"]

                @mk.sec("vector")
                def _(e):
                    mk.w(e, "qs", qs_kg)
                    mk.se(e, e.tensor_max(kng[0][:, :], kng[0][:, :], kng[1][:, :]))
                    mk.se(e, e.tensor_max(kng[2][:, :], kng[2][:, :], kng[3][:, :]))
                    i = e.tensor_max(kng[0][:, :], kng[0][:, :], kng[2][:, :])
                    mk.inc("dv", i)

                @mk.sec("scalar")
                def _(e):
                    mk.w(e, "dv", self.c["dv"])
                    mk.se(e, e.activation(out=kng[1][:, :], in_=kng[0][:, :], func=AF.Sqrt),
                          "ac")
                    i = e.activation(out=kng[2][:, :], in_=qn[:, :], func=AF.Sqrt)
                    mk.inc("ac", i)

                @mk.sec("vector")
                def _(e):
                    mk.w(e, "ac", self.c["ac"])
                    mk.se(e, e.tensor_tensor(kng[3][:, :], kng[1][:, :], kng[2][:, :],
                                             ALU.mult))
                    i = e.tensor_scalar(kng[3][:, :], kng[3][:, :], -1.0, None, ALU.mult)
                    mk.inc("dv", i)

                with ExitStack() as s2:
                    ps_nb = s2.enter_context(_ps(nc, "ab_psnb", [128, 16], F32))

                    @mk.sec("tensor")
                    def _(e):
                        mk.w(e, "dv", self.c["dv"])
                        i = e.matmul(ps_nb[:, :], sb["ones_row_f"][:, :], kng[3][:, :],
                                     start=True, stop=True)
                        mk.inc("pe", i)

                    @mk.sec("vector")
                    def _(e):
                        mk.w(e, "pe", self.c["pe"])
                        i = e.tensor_copy(negB[:, :], ps_nb[:, :])
                        mk.inc("dv", i)
                    self.barrier()

            att = q_bf
            with ExitStack() as s2:
                K_sb = [s2.enter_context(_sb(nc, f"ab_K{i}", [128, 4 * TL], BF16))
                        for i in range(2)]
                V_sb = [s2.enter_context(_sb(nc, f"ab_V{i}", [128, 4 * TL], BF16))
                        for i in range(2)]
                RING = 3
                pT = [s2.enter_context(_sb(nc, f"ab_pT{i}", [128, TH], BF16))
                      for i in range(RING)]
                sacc = s2.enter_context(_sb(nc, "ab_sacc", [128, TH], F32))
                rs_row = s2.enter_context(_sb(nc, "ab_rs", [1, TH], F32))
                rs_b = sacc
                ps_s = [s2.enter_context(_ps(nc, f"ab_pss{i}", [128, TH], F32))
                        for i in range(RING)]
                ps_o = s2.enter_context(_ps(nc, "ab_pso", [128, TH], F32))

                last_pe = {-1: 0, -2: 0}
                dv_fin = {-1: 0}
                qsk = {}
                qcv = {}

                def load_head(hh):
                    @mk.sec("sync")
                    def _(e, hh=hh):
                        if hh == 0:
                            mk.w(e, "cc", cc_kv)
                        mk.w(e, "pe", last_pe[hh - 2])
                        for r in range(4):
                            mk.dma(e, "qs", K_sb[hh % 2][:, r * TL:(r + 1) * TL],
                                   a["cc_kv_out"][r * 2 * TL + hh * 128:
                                                  r * 2 * TL + (hh + 1) * 128, :])
                    qsk[hh] = self.c["qs"]

                    @mk.sec("gpsimd")
                    def _(e, hh=hh):
                        if hh == 0:
                            mk.w(e, "cc", cc_kv)
                        mk.w(e, "pe", last_pe[hh - 2])
                        for r in range(4):
                            src = a["cc_kv_out"][r * 2 * TL + TL: (r + 1) * 2 * TL,
                                                 hh * 128:(hh + 1) * 128]
                            src = src.rearrange("(tb p) f -> p tb f", p=128)
                            dst = V_sb[hh % 2][:, r * TL:(r + 1) * TL]
                            dst = dst.rearrange("p (tb f) -> p tb f", f=128)
                            mk.dma(e, "qg", dst, src)
                    qcv[hh] = self.c["qg"]

                it = 0
                load_head(0)
                for h in range(NH):
                    if h + 1 < NH:
                        load_head(h + 1)

                    for qh in range(2):
                        pe0 = self.c["pe"]
                        ac0 = self.c["ac"]
                        dv0 = self.c["dv"]
                        q0 = h * TL + qh * TH

                        @mk.sec("tensor")
                        def _(e, h=h, qh=qh, it=it, pe0=pe0, ac0=ac0, q0=q0):
                            if qh == 0:
                                mk.w(e, "qs", qsk[h])
                                mk.w(e, "qg", qcv[h])
                            mk.w(e, "dv", dv_fin[it - 1])
                            for kb in range(64):
                                if kb >= RING:
                                    mk.w(e, "ac", ac0 + kb - RING + 1)
                                for nt in range(2):
                                    i = e.matmul(
                                        ps_s[kb % RING][:, nt * 512:(nt + 1) * 512],
                                        K_sb[h % 2][:, kb * 128:(kb + 1) * 128],
                                        q_bf[:, q0 + nt * 512: q0 + (nt + 1) * 512],
                                        start=True, stop=True, skip_group_check=True)
                                mk.inc("pe", i)
                                mk.w(e, "ac", ac0 + kb + 1)
                                for nt in range(2):
                                    i = e.matmul(
                                        ps_o[:, nt * 512:(nt + 1) * 512],
                                        V_sb[h % 2][:, kb * 128:(kb + 1) * 128],
                                        pT[kb % RING][:, nt * 512:(nt + 1) * 512],
                                        start=(kb == 0), stop=(kb == 63),
                                        skip_group_check=True)
                                mk.inc("pe", i)
                        pe_mm_end = self.c["pe"]

                        @mk.sec("scalar")
                        def _(e, h=h, pe0=pe0, dv0=dv0):
                            for kb in range(64):
                                mk.w(e, "pe", pe0 + 2 * kb + 1)
                                if kb >= RING:
                                    mk.w(e, "dv", dv0 + kb - RING + 1)
                                i = e.activation(out=pT[kb % RING][:, :],
                                                 in_=ps_s[kb % RING][:, :],
                                                 func=AF.Exp, bias=negB[:, h:h + 1])
                                mk.inc("ac", i)

                        @mk.sec("vector")
                        def _(e, ac0=ac0, dv0=dv0):
                            for kb in range(64):
                                mk.w(e, "ac", ac0 + kb + 1)
                                if kb == 0:
                                    i = e.tensor_copy(sacc[:, :], pT[0][:, :])
                                else:
                                    i = e.tensor_tensor(sacc[:, :], sacc[:, :],
                                                        pT[kb % RING][:, :], ALU.add)
                                mk.se(e, i)
                        dv_adds = self.c["dv"]

                        @mk.sec("tensor")
                        def _(e, dv_adds=dv_adds):
                            mk.w(e, "dv", dv_adds)
                            for nt in range(2):
                                i = e.matmul(ps_s[0][0:1, nt * 512:(nt + 1) * 512],
                                             sb["ones_col_f"][:, :],
                                             sacc[:, nt * 512:(nt + 1) * 512],
                                             start=True, stop=True, skip_group_check=True)
                            mk.inc("pe", i)

                        @mk.sec("vector")
                        def _(e):
                            mk.w(e, "pe", self.c["pe"])
                            i = e.reciprocal(rs_row[:, :], ps_s[0][0:1, :])
                            mk.inc("dv", i)

                        @mk.sec("tensor")
                        def _(e):
                            mk.w(e, "dv", self.c["dv"])
                            for nt in range(2):
                                i = e.matmul(ps_s[1][:, nt * 512:(nt + 1) * 512],
                                             sb["ones_row_f"][:, :],
                                             rs_row[:, nt * 512:(nt + 1) * 512],
                                             start=True, stop=True, skip_group_check=True)
                            mk.inc("pe", i)
                        last_pe[h] = self.c["pe"]

                        @mk.sec("vector")
                        def _(e, it=it, q0=q0):
                            mk.w(e, "pe", self.c["pe"])
                            mk.se(e, e.tensor_copy(rs_b[:, :], ps_s[1][:, :]))
                            i = e.tensor_tensor(att[:, q0:q0 + TH],
                                                ps_o[:, :], rs_b[:, :], ALU.mult)
                            dv_fin[it] = mk.inc("dv", i)
                        it += 1
                self.barrier()

            with ExitStack() as s3:
                wbuf2 = [s3.enter_context(_sb(nc, f"ab_wo{i}", [128, CC * 128], BF16))
                         for i in range(2)]
                o_sb = [s3.enter_context(_sb(nc, f"ab_o{i}", [128, TL], F32))
                        for i in range(2)]
                xc = [s3.enter_context(_sb(nc, f"ab_xc{i}", [128, TL], F32))
                      for i in range(2)]
                ps_op = [s3.enter_context(_ps(nc, f"ab_psop{i}", [128, TL], F32))
                         for i in range(2)]
                wld = {}; pemm = {}; acev = {}; dvres = {}; stq = {}; qx = {}
                for step in range(CC + 1):
                    if step < CC:
                        @mk.sec("gpsimd")
                        def _(e, co=step):
                            if co >= 2:
                                mk.w(e, "pe", pemm[co - 2])
                            src = a["woT"][:, co * 128:(co + 1) * 128]
                            src = src.rearrange("(cc p) h -> p cc h", p=128)
                            dst = wbuf2[co % 2][:, :].rearrange("p (cc h) -> p cc h", h=128)
                            mk.dma(e, "qg", dst, src)
                            wld[co] = self.c["qg"]

                        @mk.sec("sync")
                        def _(e, co=step):
                            if co >= 2:
                                mk.w(e, "dv", dvres[co - 2])
                            qx[co] = mk.dma(e, "qs", xc[co % 2][:, :],
                                            a["x_fm"][co * 128:(co + 1) * 128, :])
                    if step == 0:
                        continue
                    co = step - 1

                    @mk.sec("tensor")
                    def _(e, co=co):
                        mk.w(e, "qg", wld[co])
                        if co >= 2:
                            mk.w(e, "ac", acev[co - 2])
                        for cc in range(CC):
                            for nt in range(4):
                                i = e.matmul(
                                    ps_op[co % 2][:, nt * 512:(nt + 1) * 512],
                                    wbuf2[co % 2][:, cc * 128:(cc + 1) * 128],
                                    att[:, cc * TL + nt * 512: cc * TL + (nt + 1) * 512],
                                    start=(cc == 0), stop=(cc == CC - 1),
                                    skip_group_check=True)
                        pemm[co] = mk.inc("pe", i)

                    @mk.sec("scalar")
                    def _(e, co=co):
                        mk.w(e, "pe", pemm[co])
                        if co >= 2:
                            mk.w(e, "qs", stq[co - 2])
                        i = e.activation(out=o_sb[co % 2][:, :], in_=ps_op[co % 2][:, :],
                                         func=AF.Identity, bias=sb["bo"][:, co:co + 1])
                        acev[co] = mk.inc("ac", i)

                    @mk.sec("vector")
                    def _(e, co=co):
                        mk.w(e, "ac", acev[co])
                        mk.w(e, "qs", qx[co])
                        i = e.tensor_tensor(o_sb[co % 2][:, :], o_sb[co % 2][:, :],
                                            xc[co % 2][:, :], ALU.add)
                        dvres[co] = mk.inc("dv", i)

                    @mk.sec("sync")
                    def _(e, co=co):
                        mk.w(e, "dv", dvres[co])
                        stq[co] = mk.dma(e, "qs", a["x_fm"][co * 128:(co + 1) * 128, :],
                                         o_sb[co % 2][:, :])
                self.barrier()

    def finish(self):
        mk, a = self, self.a

        @mk.sec("sync")
        def _(e):
            for cc in range(CC):
                mk.dma(e, "qs", a["y_out"][cc * 128:(cc + 1) * 128, :],
                       a["x_fm"][cc * 128:(cc + 1) * 128, :])
        self.barrier()
        for en in ["sync", "gpsimd", "tensor", "vector", "scalar"]:
            @mk.sec(en)
            def _(e):
                pass


def build_program(upto=4):
    nc = bass.Bass("TRN2", target_bir_lowering=False, debug=False)

    def din(name, shape, dt=F32):
        return nc.dram_tensor(name, shape, dt, kind="ExternalInput").ap()

    aps = {
        "x_in": din("x_in", [D, TL]),
        "sel": din("sel", [1, 4]),
        "WaT": din("WaT", [D, D]),
        "WxT": din("WxT", [D, D]),
        "wqT": din("wqT", [D, D]),
        "wkT": din("wkT", [D, D]),
        "wvT": din("wvT", [D, D]),
        "woT": din("woT", [D, D]),
        "ba_c": din("ba_c", [128, 16]),
        "bx_c": din("bx_c", [128, 16]),
        "lam_c": din("lam_c", [128, 16]),
        "bq_c": din("bq_c", [128, 16]),
        "bk_c": din("bk_c", [128, 16]),
        "bo_c": din("bo_c", [128, 16]),
        "bv_row": din("bv_row", [1, D]),
        "fc1T": din("fc1T", [8, D, FM]),
        "fc2T": din("fc2T", [8, D, FM]),
        "projT": din("projT", [8, FM, D]),
        "sfc1T": din("sfc1T", [D, 4 * FM]),
        "sfc2T": din("sfc2T", [D, 4 * FM]),
        "sprojT": din("sprojT", [4 * FM, D]),
        "gate9T": din("gate9T", [128, CC * 9]),
        "ones_row_f": din("ones_row_f", [1, 128]),
        "ones_col_f": din("ones_col_f", [128, 1]),
        "constsA": din("constsA", [128, 8]),
        "ones_col_b": din("ones_col_b", [128, 1], BF16),
        "id_f": din("id_f", [128, 128]),
        "eind": din("eind", [9, 9 * 128]),
        "y_out": nc.dram_tensor("y_out", [D, TL], F32, kind="ExternalOutput").ap(),
        **({"dbg_a": nc.dram_tensor("dbg_a", [128, TL], F32, kind="ExternalOutput").ap(),
            "dbg_sq": nc.dram_tensor("dbg_sq", [128, TL], F32, kind="ExternalOutput").ap(),
            "dbg_y": nc.dram_tensor("dbg_y", [128, TL], F32, kind="ExternalOutput").ap(),
            "dbg_u": nc.dram_tensor("dbg_u", [128, TL], F32, kind="ExternalOutput").ap(),
            "dbg_r": nc.dram_tensor("dbg_r", [128, TL], F32, kind="ExternalOutput").ap(),
            "dbg_st": nc.dram_tensor("dbg_st", [128, 32], F32, kind="ExternalOutput").ap(),
            "dbg_hin": nc.dram_tensor("dbg_hin", [128, 16], F32, kind="ExternalOutput").ap(),
            "dbg_selb": nc.dram_tensor("dbg_selb", [128, 4], F32, kind="ExternalOutput").ap(),
            } if DEBUG_AA[0] is not False else {}),
        "x_fm": nc.dram_tensor("x_fm", [D, TL], F32).ap(),
        "sp_a": nc.dram_tensor("sp_a", [D, TL], F32).ap(),
        "sp_u": nc.dram_tensor("sp_u", [D, TL], F32).ap(),
        "cc_scan_in": nc.dram_tensor("cc_scan_in", [128, 32], F32).ap(),
        "cc_scan_out": nc.dram_tensor("cc_scan_out", [512, 32], F32).ap(),
        "cc_kn_in": nc.dram_tensor("cc_kn_in", [1, 16], F32).ap(),
        "cc_kn_out": nc.dram_tensor("cc_kn_out", [4, 16], F32).ap(),
        "cc_kv_in": nc.dram_tensor("cc_kv_in", [2 * TL, D], BF16).ap(),
        "cc_kv_out": nc.dram_tensor("cc_kv_out", [8 * TL, D], BF16).ap(),
    }

    with ExitStack() as es:
        block = es.enter_context(nc.Block())
        mk = MK(nc, es, block, aps)
        mk.alloc_persistent()
        mk.load_persistent()
        mk.stats_and_rows("x_in")
        mk.attn_a()
        if upto >= 2:
            mk.stats_and_rows("x_fm")
            mk.ffn("x_fm")
        if upto >= 3:
            mk.stats_and_rows("x_fm")
            mk.attn_b()
        if upto >= 4:
            mk.stats_and_rows("x_fm")
            mk.ffn("x_fm")
        mk.finish()
    return nc


def _prep_shared(i):
    f32 = np.float32
    ac = np.ascontiguousarray

    def chunk16(v):
        return ac(v.reshape(16, 128).T.astype(f32))

    sh = {
        "WaT": ac(np.asarray(i["L0_attn_a_Wa"]).T),
        "WxT": ac(np.asarray(i["L0_attn_a_Wx"]).T),
        "wqT": ac(np.asarray(i["L0_attn_b_wq"]).T),
        "wkT": ac(np.asarray(i["L0_attn_b_wk"]).T),
        "wvT": ac(np.asarray(i["L0_attn_b_wv"]).T),
        "woT": ac(np.asarray(i["L0_attn_b_wo"]).T),
        "ba_c": chunk16(np.asarray(i["L0_attn_a_ba"])),
        "bx_c": chunk16(np.asarray(i["L0_attn_a_bx"])),
        "lam_c": chunk16(np.asarray(i["L0_attn_a_Lam"])),
        "bq_c": chunk16(np.asarray(i["L0_attn_b_bq"])),
        "bk_c": chunk16(np.asarray(i["L0_attn_b_bk"])),
        "bo_c": chunk16(np.asarray(i["L0_attn_b_bo"])),
        "bv_row": ac(np.asarray(i["L0_attn_b_bv"]).reshape(1, D).astype(f32)),
        "fc1T": ac(np.asarray(i["L0_ffn_expert_fc1"]).transpose(0, 2, 1)),
        "fc2T": ac(np.asarray(i["L0_ffn_expert_fc2"]).transpose(0, 2, 1)),
        "projT": ac(np.asarray(i["L0_ffn_expert_proj"]).transpose(0, 2, 1)),
        "sfc1T": ac(np.asarray(i["L0_ffn_shared_fc1"]).T),
        "sfc2T": ac(np.asarray(i["L0_ffn_shared_fc2"]).T),
        "sprojT": ac(np.asarray(i["L0_ffn_shared_proj"]).T),
        "ones_row_f": np.ones((1, 128), f32),
        "ones_col_f": np.ones((128, 1), f32),
        "constsA": np.broadcast_to(
            np.array([1.0, 1e-5, -1.0, ATT_SCALE, 0, 0, 0, 0], f32), (128, 8)).copy(),
        "ones_col_b": np.ones((128, 1), ml_dtypes.bfloat16),
        "id_f": np.eye(128, dtype=f32),
    }
    g9 = np.concatenate([np.asarray(i["L0_ffn_gate_w"]).T,
                         np.asarray(i["L0_ffn_shared_gate_w"]).T], axis=1)
    sh["gate9T"] = ac(g9.reshape(16, 128, 9).transpose(1, 0, 2).reshape(128, 16 * 9))
    eind = np.zeros((9, 9 * 128), f32)
    for e in range(9):
        eind[e, e * 128:(e + 1) * 128] = 1.0
    sh["eind"] = eind
    return sh


def kernel(**inputs):
    x = np.asarray(inputs["L0_attn_a_x"])
    sh = _prep_shared(inputs)
    nc = build_program(upto=4)
    in_maps = []
    for c in range(8):
        b, s = c // 4, c % 4
        im = dict(sh)
        im["x_in"] = np.ascontiguousarray(x[b, s * TL:(s + 1) * TL, :].T)
        sel = np.zeros((1, 4), np.float32)
        if s > 0:
            sel[0, s - 1] = 1.0
        im["sel"] = sel
        in_maps.append(im)
    res = run_bass_kernel_spmd(nc, in_maps, core_ids=list(range(8)))
    out = np.empty((2, 8192, D), np.float32)
    for c in range(8):
        b, s = c // 4, c % 4
        out[b, s * TL:(s + 1) * TL, :] = res.results[c]["y_out"].T
    return out


# version 10
# speedup vs baseline: 1.5982x; 1.3917x over previous
import numpy as np
import ml_dtypes
import concourse.bass as bass
import concourse.mybir as mybir
from concourse.bass_utils import run_bass_kernel_spmd
from contextlib import ExitStack

F32 = mybir.dt.float32
BF16 = mybir.dt.bfloat16
AF = mybir.ActivationFunctionType
ALU = mybir.AluOpType
AX = mybir.AxisListType

D = 2048
CC = 16
TL = 2048
NH = 16
FM = 1408
FCH = 11
NEXP = 12
TH = 1024
ATT_SCALE = float(1.0 / np.sqrt(128.0))
DEBUG_AA = [False]
GROUPS = [[0, 1, 2, 3], [4, 5, 6, 7]]

_UNIQ = [0]


def _sb(nc, name, shape, dt):
    _UNIQ[0] += 1
    return nc.sbuf_tensor(f"{name}_{_UNIQ[0]}", shape, dt)


def _ps(nc, name, shape, dt):
    _UNIQ[0] += 1
    return nc.psum_tensor(f"{name}_{_UNIQ[0]}", shape, dt)


class MK:
    def __init__(self, nc, es, block, aps):
        self.nc = nc
        self.es = es
        self.blk = block
        self.a = aps
        self.S = {}
        self.c = {}
        for name in ["qs", "qc", "qg", "pe", "dv", "ac", "cc"]:
            self.S[name] = es.enter_context(nc.semaphore("sem_" + name))
            self.c[name] = 0
        self.bar = dict(self.c)
        self.bar_done = set()
        self.sb = {}

    def inc(self, name, instr, n=1):
        instr.then_inc(self.S[name], n)
        self.c[name] += n
        return self.c[name]

    def dma(self, eng, qname, out, in_):
        i = eng.dma_start(out=out, in_=in_)
        return self.inc(qname, i, 16)

    def barrier(self):
        self.bar = dict(self.c)
        self.bar_done = set()

    def w(self, eng, name, v):
        if v > 0:
            eng.wait_ge(self.S[name], v)

    def se(self, eng, instr, name="dv"):
        self.w(eng, name, self.inc(name, instr))

    def sec(self, engname):
        blkfn = getattr(self.blk, engname)

        def deco(f):
            def wrapped(e):
                if engname not in self.bar_done:
                    self.bar_done.add(engname)
                    for k, v in self.bar.items():
                        if v > 0:
                            e.wait_ge(self.S[k], v)
                f(e)
            blkfn(wrapped)
        return deco

    def alloc_persistent(self):
        nc, es, sb = self.nc, self.es, self.sb

        def t(name, shape, dt):
            sb[name] = es.enter_context(nc.sbuf_tensor("sbt_" + name, shape, dt))

        t("ones_row_f", [1, 128], F32)
        t("ones_col_f", [128, 1], F32)
        t("ones_col_b", [128, 1], BF16)
        t("id_f", [128, 128], F32)
        t("gate9", [128, CC * 9], F32)
        t("sel4", [1, 4], F32)
        t("constsA", [128, 8], F32)
        for nm in ["ba", "bx", "lam", "bq", "bk", "bo", "ngam", "bqs"]:
            t(nm, [128, 16], F32)
        t("Rb", [128, TL], F32)
        t("MRb", [128, TL], F32)

    def load_persistent(self):
        mk, sb, a = self, self.sb, self.a

        @mk.sec("sync")
        def _(e):
            for nm, src in [("ones_row_f", "ones_row_f"), ("ones_col_f", "ones_col_f"),
                            ("ones_col_b", "ones_col_b"),
                            ("id_f", "id_f"), ("gate9", "gate9T"),
                            ("sel4", "sel"), ("ba", "ba_c"), ("bx", "bx_c"),
                            ("lam", "lam_c"), ("bq", "bq_c"), ("bk", "bk_c"),
                            ("bo", "bo_c"),
                            ("constsA", "constsA")]:
                mk.dma(e, "qs", sb[nm][:, :], a[src][:, :])
        n_loads = self.c["qs"]

        ph_ps = ExitStack()
        ps_b = ph_ps.enter_context(self.nc.psum_tensor("ps_pers", [128, 512], F32))

        @mk.sec("scalar")
        def _(e):
            mk.w(e, "qs", n_loads)
            mk.se(e, e.activation(out=sb["ngam"][:, :], in_=sb["lam"][:, :], func=AF.Exp,
                                  scale=sb["constsA"][:, 2:3]), "ac")
            mk.se(e, e.activation(out=sb["ngam"][:, :], in_=sb["ngam"][:, :], func=AF.Ln,
                                  bias=sb["constsA"][:, 0:1]), "ac")
            i = e.activation(out=sb["bqs"][:, :], in_=sb["bq"][:, :], func=AF.Identity,
                             scale=sb["constsA"][:, 3:4])
            mk.inc("ac", i)

        @mk.sec("vector")
        def _(e):
            mk.w(e, "ac", self.c["ac"])
            i = e.tensor_scalar(sb["ngam"][:, :], sb["ngam"][:, :], -8.0, None, ALU.mult)
            mk.inc("dv", i)

        self.barrier()
        ph_ps.close()

    def stats_and_rows(self, src_name):
        mk, sb, a, nc = self, self.sb, self.a, self.nc
        src = a[src_name]
        with ExitStack() as ph:
            xb = [ph.enter_context(_sb(nc, f"st_xb{i}", [128, TL], BF16)) for i in range(2)]
            xq = [ph.enter_context(_sb(nc, f"st_xq{i}", [128, TL], BF16)) for i in range(2)]
            rows = {}
            for rn in ["m_row", "v_row", "tmp_row", "rstd_row", "mr_row"]:
                rows[rn] = ph.enter_context(_sb(nc, "st_" + rn, [1, TL], F32))
            with ExitStack() as ps_stk:
                ps_sm = ps_stk.enter_context(_ps(nc, "ps_sm", [1, TL], F32))
                ps_sq = ps_stk.enter_context(_ps(nc, "ps_sq", [1, TL], F32))

                ld = {}; sq = {}; mm = {}
                for cc in range(CC):
                    @mk.sec("gpsimd")
                    def _(e, cc=cc):
                        if cc >= 2:
                            mk.w(e, "pe", mm[cc - 2])
                            mk.w(e, "ac", sq[cc - 2])
                        ld[cc] = mk.dma(e, "qg", xb[cc % 2][:, :],
                                        src[cc * 128:(cc + 1) * 128, :])

                    @mk.sec("scalar")
                    def _(e, cc=cc):
                        mk.w(e, "qg", ld[cc])
                        if cc >= 2:
                            mk.w(e, "pe", mm[cc - 2])
                        i = e.square(xq[cc % 2][:, :], xb[cc % 2][:, :])
                        sq[cc] = mk.inc("ac", i)

                    @mk.sec("tensor")
                    def _(e, cc=cc):
                        mk.w(e, "qg", ld[cc])
                        mk.w(e, "ac", sq[cc])
                        for nt in range(4):
                            e.matmul(ps_sm[0:1, nt * 512:(nt + 1) * 512],
                                     sb["ones_col_b"][:, :],
                                     xb[cc % 2][:, nt * 512:(nt + 1) * 512],
                                     start=(cc == 0), stop=(cc == CC - 1),
                                     skip_group_check=True)
                            i = e.matmul(ps_sq[0:1, nt * 512:(nt + 1) * 512],
                                         sb["ones_col_b"][:, :],
                                         xq[cc % 2][:, nt * 512:(nt + 1) * 512],
                                         start=(cc == 0), stop=(cc == CC - 1),
                                         skip_group_check=True)
                        mm[cc] = mk.inc("pe", i)

                @mk.sec("vector")
                def _(e):
                    mk.w(e, "pe", mm[CC - 1])
                    e.tensor_scalar(rows["m_row"][:, :], ps_sm[0:1, :], 1.0 / D, None, ALU.mult)
                    i = e.tensor_scalar(rows["v_row"][:, :], ps_sq[0:1, :], 1.0 / D, None,
                                        ALU.mult)
                    mk.se(e, i)
                    mk.inc("dv", e.engine_nop())

                @mk.sec("scalar")
                def _(e):
                    mk.w(e, "dv", self.c["dv"])
                    i = e.square(rows["tmp_row"][:, :], rows["m_row"][:, :])
                    mk.inc("ac", i)

                @mk.sec("vector")
                def _(e):
                    mk.w(e, "ac", self.c["ac"])
                    i = e.tensor_tensor(rows["v_row"][:, :], rows["v_row"][:, :],
                                        rows["tmp_row"][:, :], ALU.subtract)
                    mk.inc("dv", i)

                @mk.sec("scalar")
                def _(e):
                    mk.w(e, "dv", self.c["dv"])
                    i = e.activation(out=rows["tmp_row"][:, :], in_=rows["v_row"][:, :],
                                     func=AF.Sqrt, bias=sb["constsA"][0:1, 1:2])
                    mk.inc("ac", i)

                @mk.sec("vector")
                def _(e):
                    mk.w(e, "ac", self.c["ac"])
                    mk.se(e, e.reciprocal(rows["rstd_row"][:, :], rows["tmp_row"][:, :]))
                    i = e.tensor_tensor(rows["mr_row"][:, :], rows["m_row"][:, :],
                                        rows["rstd_row"][:, :], ALU.mult)
                    mk.inc("dv", i)
                self.barrier()

            with ExitStack() as ps_stk:
                ps_bc = ps_stk.enter_context(_ps(nc, "ps_bc", [128, TL], F32))
                for row, dst in [("rstd_row", "Rb"), ("mr_row", "MRb")]:
                    @mk.sec("tensor")
                    def _(e, row=row):
                        mk.w(e, "dv", self.c["dv"])
                        for nt in range(4):
                            i = e.matmul(ps_bc[:, nt * 512:(nt + 1) * 512],
                                         sb["ones_row_f"][:, :],
                                         rows[row][:, nt * 512:(nt + 1) * 512],
                                         start=True, stop=True)
                        mk.inc("pe", i)

                    @mk.sec("vector")
                    def _(e, dst=dst):
                        mk.w(e, "pe", self.c["pe"])
                        i = e.tensor_copy(sb[dst][:, :], ps_bc[:, :])
                        mk.inc("dv", i)
                self.barrier()

    def emit_xn(self, xn, src, width=TL, t0=0, extra=None, xcb=None):
        mk, sb, nc = self, self.sb, self.nc
        with ExitStack() as ph:
            if xcb is None:
                xc = [ph.enter_context(_sb(nc, f"xn_xc{i}", [128, width], F32)) for i in range(2)]
            else:
                xc = xcb
            tt = ph.enter_context(_sb(nc, "xn_tt", [128, width], F32))
            tf = ph.enter_context(_sb(nc, "xn_tf", [128, width], F32))
            dvk = {}
            pe_tf = [0]
            for cc in range(CC):
                @mk.sec("sync")
                def _(e, cc=cc):
                    if cc >= 2:
                        mk.w(e, "dv", dvk[cc - 2])
                    mk.dma(e, "qs", xc[cc % 2][:, :],
                           src[cc * 128:(cc + 1) * 128, t0:t0 + width])
                qsl = self.c["qs"]

                @mk.sec("vector")
                def _(e, cc=cc, qsl=qsl):
                    mk.w(e, "qs", qsl)
                    mk.se(e, e.tensor_tensor(tt[:, :], xc[cc % 2][:, :],
                                             sb["Rb"][:, t0:t0 + width], ALU.mult))
                    if extra is None:
                        i = e.tensor_tensor(xn[:, cc * width:(cc + 1) * width], tt[:, :],
                                            sb["MRb"][:, t0:t0 + width], ALU.subtract)
                    else:
                        mk.w(e, "pe", pe_tf[0])
                        mk.se(e, e.tensor_tensor(tf[:, :], tt[:, :],
                                                 sb["MRb"][:, t0:t0 + width], ALU.subtract))
                        i = e.tensor_copy(xn[:, cc * width:(cc + 1) * width], tf[:, :])
                    dvk[cc] = mk.inc("dv", i)
                if extra is not None:
                    pe_tf[0] = extra(cc, tf, dvk[cc])
            self.barrier()

    def attn_a(self):
        mk, sb, a, nc = self, self.sb, self.a, self.nc
        with ExitStack() as ph:
            xn = ph.enter_context(_sb(nc, "aa_xn", [128, CC * TL], BF16))
            self.emit_xn(xn, a["x_in"])

            wbuf = [ph.enter_context(_sb(nc, f"aa_w{i}", [128, 2 * CC * 128], BF16))
                    for i in range(2)]
            xc = [ph.enter_context(_sb(nc, f"aa_xc{i}", [128, TL], F32)) for i in range(2)]
            tmp1 = ph.enter_context(_sb(nc, "aa_t1", [128, TL], F32))
            tmp2 = ph.enter_context(_sb(nc, "aa_t2", [128, TL], F32))
            r_sb = ph.enter_context(_sb(nc, "aa_r", [128, TL], F32))
            i_sb = ph.enter_context(_sb(nc, "aa_i", [128, TL], F32))
            a_sb = ph.enter_context(_sb(nc, "aa_a", [128, TL], F32))
            sq_sb = ph.enter_context(_sb(nc, "aa_sq", [128, TL], F32))
            u_sb = ph.enter_context(_sb(nc, "aa_u", [128, TL], F32))
            y_sb = ph.enter_context(_sb(nc, "aa_y", [128, TL], F32))
            st32 = ph.enter_context(_sb(nc, "aa_st", [128, 32], F32))
            gt = [ph.enter_context(_sb(nc, f"aa_g{i}", [128, 32], F32)) for i in range(4)]
            P1 = ph.enter_context(_sb(nc, "aa_P1", [128, 16], F32))
            P2 = ph.enter_context(_sb(nc, "aa_P2", [128, 16], F32))
            hin = ph.enter_context(_sb(nc, "aa_hin", [128, 16], F32))
            selb = ph.enter_context(_sb(nc, "aa_selb", [128, 4], F32))
            ps_r = ph.enter_context(_ps(nc, "aa_psr", [128, TL], F32))
            ps_i = ph.enter_context(_ps(nc, "aa_psi", [128, TL], F32))

            wld = {}; pemm = {}; acsig = {}; aca = {}; dvloga = {}; dvu = {}
            dvscan = {}; qsx = {}; spq = {}

            for step in range(CC + 1):
                if step < CC:
                    @mk.sec("gpsimd")
                    def _(e, hh=step):
                        if hh >= 2:
                            mk.w(e, "pe", pemm[hh - 2])
                        for mi, wn in [(0, "WaT"), (1, "WxT")]:
                            src = a[wn][:, hh * 128:(hh + 1) * 128]
                            src = src.rearrange("(cc p) h -> p cc h", p=128)
                            dst = wbuf[hh % 2][:, mi * CC * 128:(mi + 1) * CC * 128]
                            dst = dst.rearrange("p (cc h) -> p cc h", h=128)
                            mk.dma(e, "qg", dst, src)
                        wld[hh] = self.c["qg"]
                if step == 0:
                    continue
                h = step - 1

                @mk.sec("tensor")
                def _(e, h=h):
                    mk.w(e, "qg", wld[h])
                    if h >= 1:
                        mk.w(e, "ac", acsig[h - 1])
                    for mi, ps in [(0, ps_r), (1, ps_i)]:
                        for cc in range(CC):
                            for nt in range(4):
                                i = e.matmul(
                                    ps[:, nt * 512:(nt + 1) * 512],
                                    wbuf[h % 2][:, (mi * CC + cc) * 128:(mi * CC + cc + 1) * 128],
                                    xn[:, cc * TL + nt * 512: cc * TL + (nt + 1) * 512],
                                    start=(cc == 0), stop=(cc == CC - 1),
                                    skip_group_check=True)
                    pemm[h] = mk.inc("pe", i)

                @mk.sec("scalar")
                def _(e, h=h):
                    mk.w(e, "pe", pemm[h])
                    if h >= 1:
                        mk.w(e, "dv", dvu[h - 1])
                    e.activation(out=r_sb[:, :], in_=ps_r[:, :], func=AF.Sigmoid,
                                 bias=sb["ba"][:, h:h + 1])
                    i = e.activation(out=i_sb[:, :], in_=ps_i[:, :], func=AF.Sigmoid,
                                     bias=sb["bx"][:, h:h + 1])
                    acsig[h] = mk.inc("ac", i)

                @mk.sec("vector")
                def _(e, h=h):
                    mk.w(e, "ac", acsig[h])
                    i = e.tensor_scalar(tmp1[:, :], r_sb[:, :], sb["ngam"][:, h:h + 1],
                                        None, ALU.mult)
                    dvloga[h] = mk.inc("dv", i)

                @mk.sec("scalar")
                def _(e, h=h):
                    mk.w(e, "dv", dvloga[h])
                    if h >= 1:
                        mk.w(e, "qs", spq[h - 1])
                    mk.se(e, e.activation(out=a_sb[:, :], in_=tmp1[:, :], func=AF.Exp), "ac")
                    mk.se(e, e.square(tmp1[:, :], a_sb[:, :]), "ac")
                    i = e.activation(out=sq_sb[:, :], in_=tmp1[:, :], func=AF.Sqrt,
                                     scale=sb["constsA"][:, 2:3], bias=sb["constsA"][:, 0:1])
                    aca[h] = mk.inc("ac", i)

                @mk.sec("sync")
                def _(e, h=h):
                    if h >= 2:
                        mk.w(e, "dv", dvu[h - 2])
                    qsx[h] = mk.dma(e, "qs", xc[h % 2][:, :],
                                    a["x_in"][h * 128:(h + 1) * 128, :])

                @mk.sec("vector")
                def _(e, h=h):
                    mk.w(e, "qs", qsx[h])
                    mk.w(e, "ac", aca[h])
                    if h >= 1:
                        mk.w(e, "qs", spq[h - 1])
                    mk.se(e, e.tensor_tensor(tmp2[:, :], xc[h % 2][:, :], sb["Rb"][:, :],
                                             ALU.mult))
                    mk.se(e, e.tensor_tensor(tmp2[:, :], tmp2[:, :], sb["MRb"][:, :],
                                             ALU.subtract))
                    mk.se(e, e.tensor_tensor(tmp2[:, :], i_sb[:, :], tmp2[:, :], ALU.mult))
                    i = e.tensor_tensor(u_sb[:, :], tmp2[:, :], sq_sb[:, :], ALU.mult)
                    dvu[h] = mk.inc("dv", i)
                    mk.w(e, "dv", dvu[h])
                    e.tensor_reduce(st32[:, h:h + 1], a_sb[:, :], AX.X, ALU.mult)
                    i = e.tensor_tensor_scan(y_sb[:, :], a_sb[:, :], u_sb[:, :], 0.0,
                                             ALU.mult, ALU.add)
                    mk.w(e, "dv", mk.inc("dv", i))
                    i = e.tensor_copy(st32[:, 16 + h:17 + h], y_sb[:, TL - 1:TL])
                    dvscan[h] = mk.inc("dv", i)

                @mk.sec("sync")
                def _(e, h=h):
                    mk.w(e, "dv", dvscan[h])
                    mk.dma(e, "qs", a["sp_a"][h * 128:(h + 1) * 128, :], a_sb[:, :])
                    spq[h] = mk.dma(e, "qs", a["sp_u"][h * 128:(h + 1) * 128, :], u_sb[:, :])
                    if DEBUG_AA[0] is not False and h == DEBUG_AA[0]:
                        mk.dma(e, "qs", a["dbg_a"][:, :], a_sb[:, :])
                        mk.dma(e, "qs", a["dbg_u"][:, :], u_sb[:, :])
                        mk.dma(e, "qs", a["dbg_r"][:, :], i_sb[:, :])
                        mk.dma(e, "qs", a["dbg_sq"][:, :], sq_sb[:, :])
                        spq[h] = mk.dma(e, "qs", a["dbg_y"][:, :], y_sb[:, :])

            @mk.sec("sync")
            def _(e):
                mk.w(e, "dv", dvscan[CC - 1])
                mk.dma(e, "qs", a["cc_scan_in"][:, :], st32[:, :])
            qs_sc = self.c["qs"]

            @mk.sec("gpsimd")
            def _(e):
                mk.w(e, "qs", qs_sc)
                i = e.collective_compute(
                    "AllGather", ALU.bypass, replica_groups=GROUPS,
                    ins=[a["cc_scan_in"].opt()], outs=[a["cc_scan_out"].opt()])
                mk.inc("cc", i)
            cc_sc = self.c["cc"]

            @mk.sec("sync")
            def _(e):
                mk.w(e, "cc", cc_sc)
                for r in range(4):
                    mk.dma(e, "qs", gt[r][:, :], a["cc_scan_out"][r * 128:(r + 1) * 128, :])
            qs_gt = self.c["qs"]

            @mk.sec("tensor")
            def _(e):
                mk.w(e, "ac", acsig[CC - 1])
                i = e.matmul(ps_r[:, 0:4], sb["ones_row_f"][:, :], sb["sel4"][:, :],
                             start=True, stop=True)
                mk.inc("pe", i)
            pe_sel = self.c["pe"]

            @mk.sec("vector")
            def _(e):
                mk.w(e, "pe", pe_sel)
                mk.w(e, "qs", qs_gt)
                mk.se(e, e.tensor_copy(selb[:, :], ps_r[:, 0:4]))
                mk.se(e, e.tensor_tensor(P1[:, :], gt[1][:, 0:16], gt[0][:, 16:32], ALU.mult))
                mk.se(e, e.tensor_tensor(P1[:, :], P1[:, :], gt[1][:, 16:32], ALU.add))
                mk.se(e, e.tensor_tensor(P2[:, :], gt[2][:, 0:16], P1[:, :], ALU.mult))
                mk.se(e, e.tensor_tensor(P2[:, :], P2[:, :], gt[2][:, 16:32], ALU.add))
                mk.se(e, e.tensor_scalar(hin[:, :], gt[0][:, 16:32], selb[:, 0:1], None,
                                         ALU.mult))
                mk.se(e, e.tensor_scalar(gt[3][:, 0:16], P1[:, :], selb[:, 1:2], None,
                                         ALU.mult))
                mk.se(e, e.tensor_tensor(hin[:, :], hin[:, :], gt[3][:, 0:16], ALU.add))
                mk.se(e, e.tensor_scalar(gt[3][:, 0:16], P2[:, :], selb[:, 2:3], None,
                                         ALU.mult))
                i = e.tensor_tensor(hin[:, :], hin[:, :], gt[3][:, 0:16], ALU.add)
                mk.inc("dv", i)
            dv_hin = self.c["dv"]

            if DEBUG_AA[0]:
                @mk.sec("sync")
                def _(e):
                    mk.w(e, "dv", dv_hin)
                    mk.dma(e, "qs", a["dbg_st"][:, :], st32[:, :])
                    mk.dma(e, "qs", a["dbg_hin"][:, :], hin[:, :])
                    mk.dma(e, "qs", a["dbg_selb"][:, :], selb[:, :])

            ldq = {}; dvs2 = {}; stq = {}
            bufA = [(a_sb, u_sb), (r_sb, i_sb)]
            for h in range(CC):
                @mk.sec("sync")
                def _(e, h=h):
                    if h == 0:
                        mk.w(e, "dv", dv_hin)
                        mk.w(e, "qs", spq[CC - 1])
                    if h >= 2:
                        mk.w(e, "dv", dvs2[h - 2])
                    A, U = bufA[h % 2]
                    mk.dma(e, "qs", A[:, :], a["sp_a"][h * 128:(h + 1) * 128, :])
                    mk.dma(e, "qs", U[:, :], a["sp_u"][h * 128:(h + 1) * 128, :])
                    ldq[h] = mk.dma(e, "qs", xc[h % 2][:, :],
                                    a["x_in"][h * 128:(h + 1) * 128, :])

                @mk.sec("vector")
                def _(e, h=h):
                    mk.w(e, "qs", ldq[h])
                    if h >= 1:
                        mk.w(e, "qs", stq[h - 1])
                    A, U = bufA[h % 2]
                    mk.se(e, e.tensor_tensor_scan(y_sb[:, :], A[:, :], U[:, :],
                                                  hin[:, h:h + 1], ALU.mult, ALU.add))
                    i = e.tensor_tensor(y_sb[:, :], y_sb[:, :], xc[h % 2][:, :], ALU.add)
                    dvs2[h] = mk.inc("dv", i)

                @mk.sec("sync")
                def _(e, h=h):
                    mk.w(e, "dv", dvs2[h])
                    stq[h] = mk.dma(e, "qs", a["x_fm"][h * 128:(h + 1) * 128, :], y_sb[:, :])
            self.barrier()

    def ffn(self, dst_name):
        mk, sb, a, nc = self, self.sb, self.a, self.nc
        with ExitStack() as ph:
            xf = ph.enter_context(_sb(nc, "ff_xf", [128, CC * TH], BF16))
            y_sb = ph.enter_context(_sb(nc, "ff_y", [128, CC * TH], F32))
            h_e = ph.enter_context(_sb(nc, "ff_h", [128, FCH * TH], BF16))
            B_sb = ph.enter_context(_sb(nc, "ff_B", [128, TH], F32))
            cmb = ph.enter_context(_sb(nc, "ff_cmb", [9, TH], F32))
            rt_sb = ph.enter_context(_sb(nc, "ff_rt", [9, TH], F32))
            lgt = ph.enter_context(_sb(nc, "ff_lgt", [128, 9], F32))
            msk = ph.enter_context(_sb(nc, "ff_msk", [128, 16], F32))
            srow = ph.enter_context(_sb(nc, "ff_srow", [128, 2], F32))
            w1 = [ph.enter_context(_sb(nc, f"ff_w1{i}", [128, CC * 128], BF16))
                  for i in range(2)]
            w2 = [ph.enter_context(_sb(nc, f"ff_w2{i}", [128, CC * 128], BF16))
                  for i in range(2)]
            wp = [ph.enter_context(_sb(nc, f"ff_wp{i}", [128, FCH * 256], BF16))
                  for i in range(2)]
            sil = ph.enter_context(_sb(nc, "ff_sil", [128, TH], BF16))
            h2s = ph.enter_context(_sb(nc, "ff_h2s", [128, TH], BF16))
            xc = [ph.enter_context(_sb(nc, f"ff_xc{i}", [128, TH], F32)) for i in range(2)]
            eind = ph.enter_context(_sb(nc, "ff_eind", [9, 9 * 128], F32))
            P = [ph.enter_context(_ps(nc, f"ff_P{i}", [128, TH], F32)) for i in range(4)]

            @mk.sec("sync")
            def _(e):
                mk.dma(e, "qs", eind[:, :], a["eind"][:, :])

            st_half = {-1: 0, 0: 0}
            for hp in range(2):
                t0 = hp * TH
                rtk = [0]

                def router_cb(cc, tf, dvt, rtk=rtk):
                    @mk.sec("tensor")
                    def _(e, cc=cc):
                        mk.w(e, "dv", dvt)
                        if cc == 0:
                            mk.w(e, "dv", self.c["dv"])
                        for nt in range(2):
                            i = e.matmul(P[0][0:9, nt * 512:(nt + 1) * 512],
                                         sb["gate9"][:, cc * 9:(cc + 1) * 9],
                                         tf[:, nt * 512:(nt + 1) * 512],
                                         start=(cc == 0), stop=(cc == CC - 1),
                                         skip_group_check=True)
                        rtk[0] = mk.inc("pe", i)
                    return rtk[0]

                self.emit_xn(xf, a["x_fm"], width=TH, t0=t0, extra=router_cb, xcb=xc)

                @mk.sec("vector")
                def _(e):
                    mk.w(e, "pe", rtk[0])
                    i = e.tensor_copy(rt_sb[:, :], P[0][0:9, :])
                    mk.inc("dv", i)

                for tt in range(8):
                    @mk.sec("tensor")
                    def _(e, tt=tt):
                        mk.w(e, "dv", self.c["dv"])
                        i = e.transpose(P[1][:, 0:9], rt_sb[0:9, tt * 128:(tt + 1) * 128],
                                        sb["id_f"][0:9, 0:9])
                        mk.inc("pe", i)

                    @mk.sec("vector")
                    def _(e, tt=tt):
                        mk.w(e, "pe", self.c["pe"])
                        mk.se(e, e.tensor_copy(lgt[:, :], P[1][:, 0:9]))
                        mk.se(e, e.reduce_max(srow[:, 0:1], lgt[:, 0:8], AX.X))
                        i = e.tensor_scalar(srow[:, 1:2], srow[:, 0:1], -1.0, None, ALU.mult)
                        mk.inc("dv", i)

                    @mk.sec("scalar")
                    def _(e, tt=tt):
                        mk.w(e, "dv", self.c["dv"])
                        e.activation(out=lgt[:, 0:8], in_=lgt[:, 0:8], func=AF.Exp,
                                     bias=srow[:, 1:2], accum_out=srow[:, 0:1])
                        i = e.activation(out=lgt[:, 8:9], in_=lgt[:, 8:9], func=AF.Sigmoid)
                        mk.inc("ac", i)

                    @mk.sec("vector")
                    def _(e, tt=tt):
                        mk.w(e, "ac", self.c["ac"])
                        mk.se(e, e.reciprocal(srow[:, 1:2], srow[:, 0:1]))
                        mk.se(e, e.tensor_scalar(lgt[:, 0:8], lgt[:, 0:8], srow[:, 1:2], None,
                                                 ALU.mult))
                        mk.se(e, e.reduce_max(srow[:, 0:1], lgt[:, 0:8], AX.X))
                        mk.se(e, e.tensor_scalar(msk[:, 0:8], lgt[:, 0:8], srow[:, 0:1], None,
                                                 ALU.is_ge))
                        mk.se(e, e.scalar_tensor_tensor(msk[:, 8:16], msk[:, 0:8], -1e30,
                                                        lgt[:, 0:8], ALU.mult, ALU.add))
                        mk.se(e, e.reduce_max(srow[:, 1:2], msk[:, 8:16], AX.X))
                        mk.se(e, e.tensor_scalar(msk[:, 0:8], lgt[:, 0:8], srow[:, 1:2], None,
                                                 ALU.is_ge))
                        i = e.tensor_tensor(lgt[:, 0:8], lgt[:, 0:8], msk[:, 0:8], ALU.mult)
                        mk.inc("dv", i)

                    @mk.sec("tensor")
                    def _(e, tt=tt):
                        mk.w(e, "dv", self.c["dv"])
                        i = e.transpose(P[2][0:9, 0:128], lgt[:, 0:9], sb["id_f"][:, :])
                        mk.inc("pe", i)

                    @mk.sec("vector")
                    def _(e, tt=tt):
                        mk.w(e, "pe", self.c["pe"])
                        i = e.tensor_copy(cmb[:, tt * 128:(tt + 1) * 128], P[2][0:9, 0:128])
                        mk.inc("dv", i)

                pefc = {}; acsil = {}; dvh = {}
                pepj = {}; dvy = {}
                dvB = 0
                for ex in range(NEXP):
                    @mk.sec("tensor")
                    def _(e, ex=ex):
                        mk.w(e, "dv", self.c["dv"])
                        ei = ex if ex < 8 else 8
                        for nt in range(2):
                            i = e.matmul(P[3][:, nt * 512:(nt + 1) * 512],
                                         eind[:, ei * 128:(ei + 1) * 128],
                                         cmb[0:9, nt * 512:(nt + 1) * 512],
                                         start=True, stop=True, skip_group_check=True)
                        mk.inc("pe", i)

                    @mk.sec("vector")
                    def _(e, ex=ex):
                        mk.w(e, "pe", self.c["pe"])
                        i = e.tensor_copy(B_sb[:, :], P[3][:, :])
                        mk.se(e, i)
                        mk.inc("dv", e.engine_nop())
                    dvB = self.c["dv"]

                    for fc in range(FCH):
                        k = ex * FCH + fc

                        @mk.sec("gpsimd")
                        def _(e, ex=ex, fc=fc, k=k):
                            if k >= 2:
                                mk.w(e, "pe", pefc[k - 2])
                            for wn, dstb in [("fc1", w1[fc % 2]), ("fc2", w2[fc % 2])]:
                                if ex < 8:
                                    src = a[wn + "T"][ex, :, fc * 128:(fc + 1) * 128]
                                else:
                                    c0 = (ex - 8) * FM + fc * 128
                                    src = a["s" + wn + "T"][:, c0:c0 + 128]
                                src = src.rearrange("(cc p) f -> p cc f", p=128)
                                dst = dstb[:, :].rearrange("p (cc f) -> p cc f", f=128)
                                mk.dma(e, "qg", dst, src)
                        wldk = self.c["qg"]

                        @mk.sec("tensor")
                        def _(e, ex=ex, fc=fc, k=k, wldk=wldk, dvB=dvB):
                            mk.w(e, "qg", wldk)
                            if k >= 2:
                                mk.w(e, "ac", acsil[k - 2])
                                mk.w(e, "dv", dvh[k - 2])
                            else:
                                mk.w(e, "dv", dvB)
                            pA = P[0] if fc % 2 == 0 else P[2]
                            pB = P[1] if fc % 2 == 0 else P[3]
                            for wb, ps in [(w1[fc % 2], pA), (w2[fc % 2], pB)]:
                                for cc in range(CC):
                                    for nt in range(2):
                                        i = e.matmul(
                                            ps[:, nt * 512:(nt + 1) * 512],
                                            wb[:, cc * 128:(cc + 1) * 128],
                                            xf[:, cc * TH + nt * 512: cc * TH + (nt + 1) * 512],
                                            start=(cc == 0), stop=(cc == CC - 1),
                                            skip_group_check=True)
                            pefc[k] = mk.inc("pe", i)

                        @mk.sec("scalar")
                        def _(e, fc=fc, k=k):
                            mk.w(e, "pe", pefc[k])
                            if k >= 1:
                                mk.w(e, "dv", dvh[k - 1])
                            pA = P[0] if fc % 2 == 0 else P[2]
                            i = e.activation(out=sil[:, :], in_=pA[:, :], func=AF.Silu)
                            acsil[k] = mk.inc("ac", i)

                        @mk.sec("vector")
                        def _(e, fc=fc, k=k, dvB=dvB):
                            mk.w(e, "pe", pefc[k])
                            pB = P[1] if fc % 2 == 0 else P[3]
                            mk.se(e, e.tensor_tensor(h2s[:, :], pB[:, :], B_sb[:, :], ALU.mult))
                            mk.w(e, "ac", acsil[k])
                            i = e.tensor_tensor(h_e[:, fc * TH:(fc + 1) * TH], sil[:, :],
                                                h2s[:, :], ALU.mult)
                            dvh[k] = mk.inc("dv", i)

                    for q in range(8):
                        kq = ex * 8 + q

                        @mk.sec("gpsimd")
                        def _(e, ex=ex, q=q, kq=kq):
                            if kq >= 2:
                                mk.w(e, "pe", pepj[kq - 2])
                            if ex < 8:
                                src = a["projT"][ex, :, q * 256:(q + 1) * 256]
                            else:
                                src = a["sprojT"][(ex - 8) * FM:(ex - 7) * FM,
                                                  q * 256:(q + 1) * 256]
                            src = src.rearrange("(fc p) c -> p fc c", p=128)
                            dst = wp[q % 2][:, :].rearrange("p (fc c) -> p fc c", c=256)
                            mk.dma(e, "qg", dst, src)
                        wldq = self.c["qg"]

                        for cc4 in range(2):
                            ccg = q * 2 + cc4

                            @mk.sec("tensor")
                            def _(e, ex=ex, q=q, kq=kq, cc4=cc4, wldq=wldq):
                                mk.w(e, "qg", wldq)
                                mk.w(e, "dv", dvh[ex * FCH + FCH - 1])
                                ps = P[0] if cc4 % 2 == 0 else P[1]
                                war = dvy.get((ex, q - 1, cc4))
                                if war is None and ex > 0:
                                    war = dvy.get((ex - 1, 7, cc4))
                                if war:
                                    mk.w(e, "dv", war)
                                for fc in range(FCH):
                                    for nt in range(2):
                                        i = e.matmul(
                                            ps[:, nt * 512:(nt + 1) * 512],
                                            wp[q % 2][:, fc * 256 + cc4 * 128:
                                                      fc * 256 + (cc4 + 1) * 128],
                                            h_e[:, fc * TH + nt * 512: fc * TH + (nt + 1) * 512],
                                            start=(fc == 0), stop=(fc == FCH - 1),
                                            skip_group_check=True)
                                pepj[kq] = mk.inc("pe", i)

                            @mk.sec("vector")
                            def _(e, ex=ex, q=q, cc4=cc4, ccg=ccg, hp=hp):
                                mk.w(e, "pe", self.c["pe"])
                                ps = P[0] if cc4 % 2 == 0 else P[1]
                                if ex == 0:
                                    mk.w(e, "qs", st_half[hp - 1])
                                    i = e.tensor_copy(y_sb[:, ccg * TH:(ccg + 1) * TH], ps[:, :])
                                else:
                                    i = e.tensor_tensor(y_sb[:, ccg * TH:(ccg + 1) * TH],
                                                        y_sb[:, ccg * TH:(ccg + 1) * TH],
                                                        ps[:, :], ALU.add)
                                dvy[(ex, q, cc4)] = mk.inc("dv", i)

                dvres = {}
                for cc in range(CC):
                    @mk.sec("sync")
                    def _(e, cc=cc):
                        if cc >= 2:
                            mk.w(e, "dv", dvres[cc - 2])
                        mk.dma(e, "qs", xc[cc % 2][:, :],
                               a["x_fm"][cc * 128:(cc + 1) * 128, t0:t0 + TH])
                    qres = self.c["qs"]

                    @mk.sec("vector")
                    def _(e, cc=cc, qres=qres):
                        mk.w(e, "qs", qres)
                        i = e.tensor_tensor(y_sb[:, cc * TH:(cc + 1) * TH],
                                            y_sb[:, cc * TH:(cc + 1) * TH],
                                            xc[cc % 2][:, :], ALU.add)
                        dvres[cc] = mk.inc("dv", i)

                    @mk.sec("sync")
                    def _(e, cc=cc):
                        mk.w(e, "dv", dvres[cc])
                        mk.dma(e, "qs", a[dst_name][cc * 128:(cc + 1) * 128, t0:t0 + TH],
                               y_sb[:, cc * TH:(cc + 1) * TH])
                st_half[hp] = self.c["qs"]
            self.barrier()

    def attn_b(self):
        mk, sb, a, nc = self, self.sb, self.a, self.nc
        with ExitStack() as ph:
            q_bf = ph.enter_context(_sb(nc, "ab_q", [128, CC * TL], BF16))
            negB = ph.enter_context(_sb(nc, "ab_negB", [128, 16], F32))
            qn = ph.enter_context(_sb(nc, "ab_qn", [1, 16], F32))
            kn = ph.enter_context(_sb(nc, "ab_kn", [1, 16], F32))
            kng = [ph.enter_context(_sb(nc, f"ab_kng{i}", [1, 16], F32)) for i in range(4)]

            with ExitStack() as s1:
                bvb = s1.enter_context(_sb(nc, "ab_bvb", [128, D], F32))
                with ExitStack() as s0:
                    ps_b = s0.enter_context(_ps(nc, "ab_psb", [128, 512], F32))
                    bvr = s0.enter_context(_sb(nc, "ab_bvr", [1, D], F32))

                    @mk.sec("sync")
                    def _(e):
                        mk.dma(e, "qs", bvr[:, :], a["bv_row"][:, :])
                    nl = self.c["qs"]
                    for nt in range(4):
                        @mk.sec("tensor")
                        def _(e, nt=nt, nl=nl):
                            if nt == 0:
                                mk.w(e, "qs", nl)
                            else:
                                mk.w(e, "dv", self.c["dv"])
                            i = e.matmul(ps_b[:, :], sb["ones_row_f"][:, :],
                                         bvr[:, nt * 512:(nt + 1) * 512], start=True, stop=True)
                            mk.inc("pe", i)

                        @mk.sec("vector")
                        def _(e, nt=nt):
                            mk.w(e, "pe", self.c["pe"])
                            i = e.tensor_copy(bvb[:, nt * 512:(nt + 1) * 512], ps_b[:, :])
                            mk.inc("dv", i)
                    self.barrier()


                xn = s1.enter_context(_sb(nc, "ab_xn", [128, CC * TL], BF16))
                self.emit_xn(xn, a["x_fm"])

                wbuf = [s1.enter_context(_sb(nc, f"ab_w{i}", [128, CC * 128], BF16))
                        for i in range(2)]
                kev = [s1.enter_context(_sb(nc, f"ab_kev{i}", [128, TL], BF16))
                       for i in range(2)]
                ksq = s1.enter_context(_sb(nc, "ab_ksq", [128, TL], BF16))
                wv = s1.enter_context(_sb(nc, "ab_wv", [128, CC * 512], BF16))

                def proj_loop(wname, brow, scale, store, qdst, nrow):
                    wld = {}; pemm = {}; acev = {}; dvnm = {}; stq = {}
                    with ExitStack() as s2:
                        ps_k = s2.enter_context(_ps(nc, "ab_psk", [128, TL], F32))
                        ps_n = s2.enter_context(_ps(nc, "ab_psn", [1, TL], F32))
                        for step in range(CC + 1):
                            if step < CC:
                                @mk.sec("gpsimd")
                                def _(e, hh=step):
                                    if hh >= 2:
                                        mk.w(e, "pe", pemm[hh - 2])
                                    src = a[wname][:, hh * 128:(hh + 1) * 128]
                                    src = src.rearrange("(cc p) h -> p cc h", p=128)
                                    dst = wbuf[hh % 2][:, :].rearrange("p (cc h) -> p cc h",
                                                                       h=128)
                                    mk.dma(e, "qg", dst, src)
                                    wld[hh] = self.c["qg"]
                            if step == 0:
                                continue
                            h = step - 1

                            @mk.sec("tensor")
                            def _(e, h=h):
                                mk.w(e, "qg", wld[h])
                                if h >= 1:
                                    mk.w(e, "ac", acev[h - 1])
                                for cc in range(CC):
                                    for nt in range(4):
                                        i = e.matmul(
                                            ps_k[:, nt * 512:(nt + 1) * 512],
                                            wbuf[h % 2][:, cc * 128:(cc + 1) * 128],
                                            xn[:, cc * TL + nt * 512: cc * TL + (nt + 1) * 512],
                                            start=(cc == 0), stop=(cc == CC - 1),
                                            skip_group_check=True)
                                pemm[h] = mk.inc("pe", i)

                            @mk.sec("scalar")
                            def _(e, h=h):
                                mk.w(e, "pe", pemm[h])
                                if h >= 2 and store:
                                    mk.w(e, "qs", stq[h - 2])
                                if h >= 1:
                                    mk.w(e, "dv", dvnm[h - 1])
                                dst = (kev[h % 2][:, :] if store
                                       else q_bf[:, h * TL:(h + 1) * TL])
                                if scale is None:
                                    i = e.activation(out=dst, in_=ps_k[:, :], func=AF.Identity,
                                                     bias=brow[:, h:h + 1])
                                else:
                                    i = e.activation(out=dst, in_=ps_k[:, :], func=AF.Identity,
                                                     scale=scale, bias=brow[:, h:h + 1])
                                mk.se(e, i, "ac")
                                i = e.square(ksq[:, :], dst)
                                acev[h] = mk.inc("ac", i)

                            @mk.sec("tensor")
                            def _(e, h=h):
                                mk.w(e, "ac", acev[h])
                                if h >= 1:
                                    mk.w(e, "dv", dvnm[h - 1])
                                for nt in range(4):
                                    i = e.matmul(ps_n[0:1, nt * 512:(nt + 1) * 512],
                                                 sb["ones_col_b"][:, :],
                                                 ksq[:, nt * 512:(nt + 1) * 512],
                                                 start=True, stop=True, skip_group_check=True)
                                mk.inc("pe", i)

                            @mk.sec("vector")
                            def _(e, h=h):
                                mk.w(e, "pe", self.c["pe"])
                                i = e.reduce_max(nrow[0:1, h:h + 1], ps_n[0:1, :], AX.X)
                                dvnm[h] = mk.inc("dv", i)

                            if store:
                                @mk.sec("sync")
                                def _(e, h=h):
                                    mk.w(e, "ac", acev[h])
                                    stq[h] = mk.dma(e, "qs",
                                                    a["cc_kv_in"][h * 128:(h + 1) * 128, :],
                                                    kev[h % 2][:, :])
                        self.barrier()

                proj_loop("wkT", sb["bk"], None, True, None, kn)

                @mk.sec("sync")
                def _(e):
                    mk.dma(e, "qs", a["cc_kn_in"][:, :], kn[:, :])
                qs_kn = self.c["qs"]

                @mk.sec("gpsimd")
                def _(e):
                    mk.w(e, "qs", qs_kn)
                    i = e.collective_compute(
                        "AllGather", ALU.bypass, replica_groups=GROUPS,
                        ins=[a["cc_kn_in"].opt()], outs=[a["cc_kn_out"].opt()])
                    mk.inc("cc", i)
                cc_kn = self.c["cc"]

                with ExitStack() as s2:
                    ps_v = [s2.enter_context(_ps(nc, f"ab_psv{i}", [128, 512], F32))
                            for i in range(2)]
                    wvld = {}; pev = {}; dvv = {}; stv = {}
                    for chb in range(4):
                        @mk.sec("gpsimd")
                        def _(e, chb=chb):
                            if chb >= 1:
                                mk.w(e, "pe", pev[(chb - 1, CC - 1)])
                            src = a["wvT"][:, chb * 512:(chb + 1) * 512]
                            src = src.rearrange("(cc p) h -> p cc h", p=128)
                            dst = wv[:, :].rearrange("p (cc h) -> p cc h", h=512)
                            mk.dma(e, "qg", dst, src)
                        wvld[chb] = self.c["qg"]

                        for tt in range(CC):
                            @mk.sec("tensor")
                            def _(e, chb=chb, tt=tt):
                                mk.w(e, "qg", wvld[chb])
                                war = dvv.get((chb, tt - 2)) or (dvv.get((chb - 1, CC - 2 + tt))
                                                                 if chb >= 1 else 0)
                                if war:
                                    mk.w(e, "dv", war)
                                for cc in range(CC):
                                    i = e.matmul(
                                        ps_v[tt % 2][:, :],
                                        xn[:, cc * TL + tt * 128: cc * TL + (tt + 1) * 128],
                                        wv[:, cc * 512:(cc + 1) * 512],
                                        start=(cc == 0), stop=(cc == CC - 1),
                                        skip_group_check=True)
                                pev[(chb, tt)] = mk.inc("pe", i)

                            @mk.sec("vector")
                            def _(e, chb=chb, tt=tt):
                                mk.w(e, "pe", pev[(chb, tt)])
                                if stv.get((chb, tt - 2)) or (chb >= 1):
                                    mk.w(e, "qs", stv.get((chb, tt - 2),
                                                          stv.get((chb - 1, CC - 2 + tt), 0)))
                                i = e.tensor_tensor(kev[tt % 2][:, 0:512], ps_v[tt % 2][:, :],
                                                    bvb[:, chb * 512:(chb + 1) * 512],
                                                    ALU.add)
                                dvv[(chb, tt)] = mk.inc("dv", i)

                            @mk.sec("sync")
                            def _(e, chb=chb, tt=tt):
                                mk.w(e, "dv", dvv[(chb, tt)])
                                stv[(chb, tt)] = mk.dma(
                                    e, "qs",
                                    a["cc_kv_in"][TL + tt * 128:TL + (tt + 1) * 128,
                                                  chb * 512:(chb + 1) * 512],
                                    kev[tt % 2][:, 0:512])
                    self.barrier()

                @mk.sec("gpsimd")
                def _(e):
                    i = e.collective_compute(
                        "AllGather", ALU.bypass, replica_groups=GROUPS,
                        ins=[a["cc_kv_in"].opt()], outs=[a["cc_kv_out"].opt()])
                    mk.inc("cc", i)
                cc_kv = self.c["cc"]

                proj_loop("wqT", sb["bqs"], sb["constsA"][:, 3:4], False, q_bf, qn)

                @mk.sec("sync")
                def _(e):
                    mk.w(e, "cc", cc_kn)
                    for r in range(4):
                        mk.dma(e, "qs", kng[r][:, :], a["cc_kn_out"][r:r + 1, :])
                qs_kg = self.c["qs"]

                @mk.sec("vector")
                def _(e):
                    mk.w(e, "qs", qs_kg)
                    mk.se(e, e.tensor_max(kng[0][:, :], kng[0][:, :], kng[1][:, :]))
                    mk.se(e, e.tensor_max(kng[2][:, :], kng[2][:, :], kng[3][:, :]))
                    i = e.tensor_max(kng[0][:, :], kng[0][:, :], kng[2][:, :])
                    mk.inc("dv", i)

                @mk.sec("scalar")
                def _(e):
                    mk.w(e, "dv", self.c["dv"])
                    mk.se(e, e.activation(out=kng[1][:, :], in_=kng[0][:, :], func=AF.Sqrt),
                          "ac")
                    i = e.activation(out=kng[2][:, :], in_=qn[:, :], func=AF.Sqrt)
                    mk.inc("ac", i)

                @mk.sec("vector")
                def _(e):
                    mk.w(e, "ac", self.c["ac"])
                    mk.se(e, e.tensor_tensor(kng[3][:, :], kng[1][:, :], kng[2][:, :],
                                             ALU.mult))
                    i = e.tensor_scalar(kng[3][:, :], kng[3][:, :], -1.0, None, ALU.mult)
                    mk.inc("dv", i)

                with ExitStack() as s2:
                    ps_nb = s2.enter_context(_ps(nc, "ab_psnb", [128, 16], F32))

                    @mk.sec("tensor")
                    def _(e):
                        mk.w(e, "dv", self.c["dv"])
                        i = e.matmul(ps_nb[:, :], sb["ones_row_f"][:, :], kng[3][:, :],
                                     start=True, stop=True)
                        mk.inc("pe", i)

                    @mk.sec("vector")
                    def _(e):
                        mk.w(e, "pe", self.c["pe"])
                        i = e.tensor_copy(negB[:, :], ps_nb[:, :])
                        mk.inc("dv", i)
                    self.barrier()

            att = q_bf
            with ExitStack() as s2:
                K_sb = [s2.enter_context(_sb(nc, f"ab_K{i}", [128, 4 * TL], BF16))
                        for i in range(2)]
                V_sb = [s2.enter_context(_sb(nc, f"ab_V{i}", [128, 4 * TL], BF16))
                        for i in range(2)]
                RING = 3
                pT = [s2.enter_context(_sb(nc, f"ab_pT{i}", [128, TH], BF16))
                      for i in range(RING)]
                sacc = s2.enter_context(_sb(nc, "ab_sacc", [128, TH], F32))
                rs_row = s2.enter_context(_sb(nc, "ab_rs", [1, TH], F32))
                rs_b = sacc
                ps_s = [s2.enter_context(_ps(nc, f"ab_pss{i}", [128, TH], F32))
                        for i in range(RING)]
                ps_o = s2.enter_context(_ps(nc, "ab_pso", [128, TH], F32))

                last_pe = {-1: 0, -2: 0}
                dv_fin = {-1: 0}
                qsk = {}
                qcv = {}

                def load_head(hh):
                    @mk.sec("sync")
                    def _(e, hh=hh):
                        if hh == 0:
                            mk.w(e, "cc", cc_kv)
                        mk.w(e, "pe", last_pe[hh - 2])
                        for r in range(4):
                            mk.dma(e, "qs", K_sb[hh % 2][:, r * TL:(r + 1) * TL],
                                   a["cc_kv_out"][r * 2 * TL + hh * 128:
                                                  r * 2 * TL + (hh + 1) * 128, :])
                    qsk[hh] = self.c["qs"]

                    @mk.sec("gpsimd")
                    def _(e, hh=hh):
                        if hh == 0:
                            mk.w(e, "cc", cc_kv)
                        mk.w(e, "pe", last_pe[hh - 2])
                        for r in range(4):
                            src = a["cc_kv_out"][r * 2 * TL + TL: (r + 1) * 2 * TL,
                                                 hh * 128:(hh + 1) * 128]
                            src = src.rearrange("(tb p) f -> p tb f", p=128)
                            dst = V_sb[hh % 2][:, r * TL:(r + 1) * TL]
                            dst = dst.rearrange("p (tb f) -> p tb f", f=128)
                            mk.dma(e, "qg", dst, src)
                    qcv[hh] = self.c["qg"]

                it = 0
                load_head(0)
                for h in range(NH):
                    if h + 1 < NH:
                        load_head(h + 1)

                    for qh in range(2):
                        pe0 = self.c["pe"]
                        ac0 = self.c["ac"]
                        dv0 = self.c["dv"]
                        q0 = h * TL + qh * TH

                        LEAD = 2
                        pe_s = {}
                        pe_v = {}

                        @mk.sec("tensor")
                        def _(e, h=h, qh=qh, it=it, pe0=pe0, ac0=ac0, q0=q0):
                            if qh == 0:
                                mk.w(e, "qs", qsk[h])
                                mk.w(e, "qg", qcv[h])
                            mk.w(e, "dv", dv_fin[it - 1])
                            for j in range(64 + LEAD):
                                if j < 64:
                                    kb = j
                                    if kb >= RING:
                                        mk.w(e, "ac", ac0 + kb - RING + 1)
                                    for nt in range(2):
                                        i = e.matmul(
                                            ps_s[kb % RING][:, nt * 512:(nt + 1) * 512],
                                            K_sb[h % 2][:, kb * 128:(kb + 1) * 128],
                                            q_bf[:, q0 + nt * 512: q0 + (nt + 1) * 512],
                                            start=True, stop=True, skip_group_check=True)
                                    pe_s[kb] = mk.inc("pe", i)
                                if j >= LEAD:
                                    kb = j - LEAD
                                    mk.w(e, "ac", ac0 + kb + 1)
                                    for nt in range(2):
                                        i = e.matmul(
                                            ps_o[:, nt * 512:(nt + 1) * 512],
                                            V_sb[h % 2][:, kb * 128:(kb + 1) * 128],
                                            pT[kb % RING][:, nt * 512:(nt + 1) * 512],
                                            start=(kb == 0), stop=(kb == 63),
                                            skip_group_check=True)
                                    pe_v[kb] = mk.inc("pe", i)

                        @mk.sec("scalar")
                        def _(e, h=h, pe0=pe0, dv0=dv0):
                            for kb in range(64):
                                mk.w(e, "pe", pe_s[kb])
                                if kb >= RING:
                                    mk.w(e, "pe", pe_v[kb - RING])
                                    mk.w(e, "dv", dv0 + kb - RING + 1)
                                i = e.activation(out=pT[kb % RING][:, :],
                                                 in_=ps_s[kb % RING][:, :],
                                                 func=AF.Exp, bias=negB[:, h:h + 1])
                                mk.inc("ac", i)

                        @mk.sec("vector")
                        def _(e, ac0=ac0, dv0=dv0):
                            for kb in range(64):
                                mk.w(e, "ac", ac0 + kb + 1)
                                if kb == 0:
                                    i = e.tensor_copy(sacc[:, :], pT[0][:, :])
                                else:
                                    i = e.tensor_tensor(sacc[:, :], sacc[:, :],
                                                        pT[kb % RING][:, :], ALU.add)
                                mk.se(e, i)
                        dv_adds = self.c["dv"]

                        @mk.sec("tensor")
                        def _(e, dv_adds=dv_adds):
                            mk.w(e, "dv", dv_adds)
                            for nt in range(2):
                                i = e.matmul(ps_s[0][0:1, nt * 512:(nt + 1) * 512],
                                             sb["ones_col_f"][:, :],
                                             sacc[:, nt * 512:(nt + 1) * 512],
                                             start=True, stop=True, skip_group_check=True)
                            mk.inc("pe", i)

                        @mk.sec("vector")
                        def _(e):
                            mk.w(e, "pe", self.c["pe"])
                            i = e.reciprocal(rs_row[:, :], ps_s[0][0:1, :])
                            mk.inc("dv", i)

                        @mk.sec("tensor")
                        def _(e):
                            mk.w(e, "dv", self.c["dv"])
                            for nt in range(2):
                                i = e.matmul(ps_s[1][:, nt * 512:(nt + 1) * 512],
                                             sb["ones_row_f"][:, :],
                                             rs_row[:, nt * 512:(nt + 1) * 512],
                                             start=True, stop=True, skip_group_check=True)
                            mk.inc("pe", i)
                        last_pe[h] = self.c["pe"]

                        @mk.sec("vector")
                        def _(e, it=it, q0=q0):
                            mk.w(e, "pe", self.c["pe"])
                            mk.se(e, e.tensor_copy(rs_b[:, :], ps_s[1][:, :]))
                            i = e.tensor_tensor(att[:, q0:q0 + TH],
                                                ps_o[:, :], rs_b[:, :], ALU.mult)
                            dv_fin[it] = mk.inc("dv", i)
                        it += 1
                self.barrier()

            with ExitStack() as s3:
                wbuf2 = [s3.enter_context(_sb(nc, f"ab_wo{i}", [128, CC * 128], BF16))
                         for i in range(2)]
                o_sb = [s3.enter_context(_sb(nc, f"ab_o{i}", [128, TL], F32))
                        for i in range(2)]
                xc = [s3.enter_context(_sb(nc, f"ab_xc{i}", [128, TL], F32))
                      for i in range(2)]
                ps_op = [s3.enter_context(_ps(nc, f"ab_psop{i}", [128, TL], F32))
                         for i in range(2)]
                wld = {}; pemm = {}; acev = {}; dvres = {}; stq = {}; qx = {}
                for step in range(CC + 1):
                    if step < CC:
                        @mk.sec("gpsimd")
                        def _(e, co=step):
                            if co >= 2:
                                mk.w(e, "pe", pemm[co - 2])
                            src = a["woT"][:, co * 128:(co + 1) * 128]
                            src = src.rearrange("(cc p) h -> p cc h", p=128)
                            dst = wbuf2[co % 2][:, :].rearrange("p (cc h) -> p cc h", h=128)
                            mk.dma(e, "qg", dst, src)
                            wld[co] = self.c["qg"]

                        @mk.sec("sync")
                        def _(e, co=step):
                            if co >= 2:
                                mk.w(e, "dv", dvres[co - 2])
                            qx[co] = mk.dma(e, "qs", xc[co % 2][:, :],
                                            a["x_fm"][co * 128:(co + 1) * 128, :])
                    if step == 0:
                        continue
                    co = step - 1

                    @mk.sec("tensor")
                    def _(e, co=co):
                        mk.w(e, "qg", wld[co])
                        if co >= 2:
                            mk.w(e, "ac", acev[co - 2])
                        for cc in range(CC):
                            for nt in range(4):
                                i = e.matmul(
                                    ps_op[co % 2][:, nt * 512:(nt + 1) * 512],
                                    wbuf2[co % 2][:, cc * 128:(cc + 1) * 128],
                                    att[:, cc * TL + nt * 512: cc * TL + (nt + 1) * 512],
                                    start=(cc == 0), stop=(cc == CC - 1),
                                    skip_group_check=True)
                        pemm[co] = mk.inc("pe", i)

                    @mk.sec("scalar")
                    def _(e, co=co):
                        mk.w(e, "pe", pemm[co])
                        if co >= 2:
                            mk.w(e, "qs", stq[co - 2])
                        i = e.activation(out=o_sb[co % 2][:, :], in_=ps_op[co % 2][:, :],
                                         func=AF.Identity, bias=sb["bo"][:, co:co + 1])
                        acev[co] = mk.inc("ac", i)

                    @mk.sec("vector")
                    def _(e, co=co):
                        mk.w(e, "ac", acev[co])
                        mk.w(e, "qs", qx[co])
                        i = e.tensor_tensor(o_sb[co % 2][:, :], o_sb[co % 2][:, :],
                                            xc[co % 2][:, :], ALU.add)
                        dvres[co] = mk.inc("dv", i)

                    @mk.sec("sync")
                    def _(e, co=co):
                        mk.w(e, "dv", dvres[co])
                        stq[co] = mk.dma(e, "qs", a["x_fm"][co * 128:(co + 1) * 128, :],
                                         o_sb[co % 2][:, :])
                self.barrier()

    def finish(self):
        mk, a = self, self.a

        @mk.sec("sync")
        def _(e):
            for cc in range(CC):
                mk.dma(e, "qs", a["y_out"][cc * 128:(cc + 1) * 128, :],
                       a["x_fm"][cc * 128:(cc + 1) * 128, :])
        self.barrier()
        for en in ["sync", "gpsimd", "tensor", "vector", "scalar"]:
            @mk.sec(en)
            def _(e):
                pass


def build_program(upto=4):
    nc = bass.Bass("TRN2", target_bir_lowering=False, debug=False)

    def din(name, shape, dt=F32):
        return nc.dram_tensor(name, shape, dt, kind="ExternalInput").ap()

    aps = {
        "x_in": din("x_in", [D, TL]),
        "sel": din("sel", [1, 4]),
        "WaT": din("WaT", [D, D]),
        "WxT": din("WxT", [D, D]),
        "wqT": din("wqT", [D, D]),
        "wkT": din("wkT", [D, D]),
        "wvT": din("wvT", [D, D]),
        "woT": din("woT", [D, D]),
        "ba_c": din("ba_c", [128, 16]),
        "bx_c": din("bx_c", [128, 16]),
        "lam_c": din("lam_c", [128, 16]),
        "bq_c": din("bq_c", [128, 16]),
        "bk_c": din("bk_c", [128, 16]),
        "bo_c": din("bo_c", [128, 16]),
        "bv_row": din("bv_row", [1, D]),
        "fc1T": din("fc1T", [8, D, FM]),
        "fc2T": din("fc2T", [8, D, FM]),
        "projT": din("projT", [8, FM, D]),
        "sfc1T": din("sfc1T", [D, 4 * FM]),
        "sfc2T": din("sfc2T", [D, 4 * FM]),
        "sprojT": din("sprojT", [4 * FM, D]),
        "gate9T": din("gate9T", [128, CC * 9]),
        "ones_row_f": din("ones_row_f", [1, 128]),
        "ones_col_f": din("ones_col_f", [128, 1]),
        "constsA": din("constsA", [128, 8]),
        "ones_col_b": din("ones_col_b", [128, 1], BF16),
        "id_f": din("id_f", [128, 128]),
        "eind": din("eind", [9, 9 * 128]),
        "y_out": nc.dram_tensor("y_out", [D, TL], F32, kind="ExternalOutput").ap(),
        **({"dbg_a": nc.dram_tensor("dbg_a", [128, TL], F32, kind="ExternalOutput").ap(),
            "dbg_sq": nc.dram_tensor("dbg_sq", [128, TL], F32, kind="ExternalOutput").ap(),
            "dbg_y": nc.dram_tensor("dbg_y", [128, TL], F32, kind="ExternalOutput").ap(),
            "dbg_u": nc.dram_tensor("dbg_u", [128, TL], F32, kind="ExternalOutput").ap(),
            "dbg_r": nc.dram_tensor("dbg_r", [128, TL], F32, kind="ExternalOutput").ap(),
            "dbg_st": nc.dram_tensor("dbg_st", [128, 32], F32, kind="ExternalOutput").ap(),
            "dbg_hin": nc.dram_tensor("dbg_hin", [128, 16], F32, kind="ExternalOutput").ap(),
            "dbg_selb": nc.dram_tensor("dbg_selb", [128, 4], F32, kind="ExternalOutput").ap(),
            } if DEBUG_AA[0] is not False else {}),
        "x_fm": nc.dram_tensor("x_fm", [D, TL], F32).ap(),
        "sp_a": nc.dram_tensor("sp_a", [D, TL], F32).ap(),
        "sp_u": nc.dram_tensor("sp_u", [D, TL], F32).ap(),
        "cc_scan_in": nc.dram_tensor("cc_scan_in", [128, 32], F32).ap(),
        "cc_scan_out": nc.dram_tensor("cc_scan_out", [512, 32], F32).ap(),
        "cc_kn_in": nc.dram_tensor("cc_kn_in", [1, 16], F32).ap(),
        "cc_kn_out": nc.dram_tensor("cc_kn_out", [4, 16], F32).ap(),
        "cc_kv_in": nc.dram_tensor("cc_kv_in", [2 * TL, D], BF16).ap(),
        "cc_kv_out": nc.dram_tensor("cc_kv_out", [8 * TL, D], BF16).ap(),
    }

    with ExitStack() as es:
        block = es.enter_context(nc.Block())
        mk = MK(nc, es, block, aps)
        mk.alloc_persistent()
        mk.load_persistent()
        mk.stats_and_rows("x_in")
        mk.attn_a()
        if upto >= 2:
            mk.stats_and_rows("x_fm")
            mk.ffn("x_fm")
        if upto >= 3:
            mk.stats_and_rows("x_fm")
            mk.attn_b()
        if upto >= 4:
            mk.stats_and_rows("x_fm")
            mk.ffn("x_fm")
        mk.finish()
    return nc


def _prep_shared(i):
    f32 = np.float32
    ac = np.ascontiguousarray

    def chunk16(v):
        return ac(v.reshape(16, 128).T.astype(f32))

    sh = {
        "WaT": ac(np.asarray(i["L0_attn_a_Wa"]).T),
        "WxT": ac(np.asarray(i["L0_attn_a_Wx"]).T),
        "wqT": ac(np.asarray(i["L0_attn_b_wq"]).T),
        "wkT": ac(np.asarray(i["L0_attn_b_wk"]).T),
        "wvT": ac(np.asarray(i["L0_attn_b_wv"]).T),
        "woT": ac(np.asarray(i["L0_attn_b_wo"]).T),
        "ba_c": chunk16(np.asarray(i["L0_attn_a_ba"])),
        "bx_c": chunk16(np.asarray(i["L0_attn_a_bx"])),
        "lam_c": chunk16(np.asarray(i["L0_attn_a_Lam"])),
        "bq_c": chunk16(np.asarray(i["L0_attn_b_bq"])),
        "bk_c": chunk16(np.asarray(i["L0_attn_b_bk"])),
        "bo_c": chunk16(np.asarray(i["L0_attn_b_bo"])),
        "bv_row": ac(np.asarray(i["L0_attn_b_bv"]).reshape(1, D).astype(f32)),
        "fc1T": ac(np.asarray(i["L0_ffn_expert_fc1"]).transpose(0, 2, 1)),
        "fc2T": ac(np.asarray(i["L0_ffn_expert_fc2"]).transpose(0, 2, 1)),
        "projT": ac(np.asarray(i["L0_ffn_expert_proj"]).transpose(0, 2, 1)),
        "sfc1T": ac(np.asarray(i["L0_ffn_shared_fc1"]).T),
        "sfc2T": ac(np.asarray(i["L0_ffn_shared_fc2"]).T),
        "sprojT": ac(np.asarray(i["L0_ffn_shared_proj"]).T),
        "ones_row_f": np.ones((1, 128), f32),
        "ones_col_f": np.ones((128, 1), f32),
        "constsA": np.broadcast_to(
            np.array([1.0, 1e-5, -1.0, ATT_SCALE, 0, 0, 0, 0], f32), (128, 8)).copy(),
        "ones_col_b": np.ones((128, 1), ml_dtypes.bfloat16),
        "id_f": np.eye(128, dtype=f32),
    }
    g9 = np.concatenate([np.asarray(i["L0_ffn_gate_w"]).T,
                         np.asarray(i["L0_ffn_shared_gate_w"]).T], axis=1)
    sh["gate9T"] = ac(g9.reshape(16, 128, 9).transpose(1, 0, 2).reshape(128, 16 * 9))
    eind = np.zeros((9, 9 * 128), f32)
    for e in range(9):
        eind[e, e * 128:(e + 1) * 128] = 1.0
    sh["eind"] = eind
    return sh


def kernel(**inputs):
    x = np.asarray(inputs["L0_attn_a_x"])
    sh = _prep_shared(inputs)
    nc = build_program(upto=4)
    in_maps = []
    for c in range(8):
        b, s = c // 4, c % 4
        im = dict(sh)
        im["x_in"] = np.ascontiguousarray(x[b, s * TL:(s + 1) * TL, :].T)
        sel = np.zeros((1, 4), np.float32)
        if s > 0:
            sel[0, s - 1] = 1.0
        im["sel"] = sel
        in_maps.append(im)
    res = run_bass_kernel_spmd(nc, in_maps, core_ids=list(range(8)))
    out = np.empty((2, 8192, D), np.float32)
    for c in range(8):
        b, s = c // 4, c % 4
        out[b, s * TL:(s + 1) * TL, :] = res.results[c]["y_out"].T
    return out
